# version 10
# baseline (speedup 1.0000x reference)
import math
import os
from contextlib import ExitStack

import numpy as np
import ml_dtypes

import concourse.bass as bass
import concourse.mybir as mybir
from concourse.bass_utils import run_bass_kernel_spmd

F32 = mybir.dt.float32
BF16 = mybir.dt.bfloat16
AF = mybir.ActivationFunctionType
ALU = mybir.AluOpType

NCORE = 8
D = 4096
ND = 32
T = 8192
TC = 1024
NH = 32
NKV = 8
HD = 128
NJ = 4
QT = 256
EPS = 1e-5
NQMASK = 7580.0

_PHASES = int(os.environ.get("MK_PHASES", "5"))


def build(ph_max=5):
    nc = bass.Bass("TRN2", target_bir_lowering=False)

    ap = {}
    def din(name, shape, dt=F32):
        ap[name] = nc.dram_tensor(name, shape, dt, kind="ExternalInput").ap()
    def dout(name, shape, dt=F32):
        ap[name] = nc.dram_tensor(name, shape, dt, kind="ExternalOutput").ap()
    def dint(name, shape, dt=F32):
        ap[name] = nc.dram_tensor(name, shape, dt).ap()

    din("xT", [D, TC])
    din("wfT", [D, D]); din("wqT", [D, D]); din("woT", [D, D])
    din("wkT", [D, NKV * HD]); din("wvT", [D, NKV * HD])
    din("bcols", [128, ND])
    din("aT", [D, 128]); din("bT", [128, D])
    din("rwT", [D, 8]); din("rb8", [8, 1]); din("esel", [8, 128])
    din("cosk", [128, TC]); din("sink", [128, TC])
    din("wrow", [1, TC]); din("ident", [128, 128])
    din("masksd", [NJ, 128, 16 * QT], BF16)
    dout("outT", [D, TC])
    dint("x1d", [D, TC]); dint("x2d", [D, TC])
    dint("qd", [NH * HD, TC], BF16)
    dint("kvin", [2 * NKV * TC, HD], BF16)
    dint("kTd", [NKV, HD, T], BF16)
    dint("kvout", [NCORE * 2 * NKV * TC, HD], BF16)
    dint("ar1i", [128, ND]); dint("ar1o", [128, ND])
    dint("ar2i", [128, ND]); dint("ar2o", [128, ND])
    dint("arki", [1, 8]); dint("arko", [1, 8])
    dint("rowsd", [4, TC])

    st = ExitStack()
    S = {}
    def sbt(name, shape, dt):
        S[name] = st.enter_context(nc.sbuf_tensor(name, shape, dt))
    PS = {}
    def pst(name, shape, dt):
        PS[name] = st.enter_context(nc.psum_tensor(name, shape, dt))
    SEMH = {}
    CNT = {}
    def mksem(name):
        SEMH[name] = st.enter_context(nc.semaphore(name))
        CNT[name] = 0

    with st:
        sbt("xln", [128, ND * TC], BF16)
        sbt("slab0", [128, D], BF16); sbt("slab1", [128, D], BF16)
        sbt("abf", [128, ND * 128], BF16)
        sbt("btf", [128, D], BF16)
        sbt("rwbf", [128, ND * 8], BF16)
        sbt("eselbf", [8, 128], BF16)
        sbt("maskb", [128, 16 * QT], BF16)
        sbt("s_cosk", [128, TC], F32); sbt("s_sink", [128, TC], F32)
        sbt("mb", [128, TC], F32); sbt("rb_", [128, TC], F32)
        sbt("wb", [128, TC], F32); sbt("wrb", [128, TC], F32)
        sbt("xs0", [128, TC], F32); sbt("xs1", [128, TC], F32)
        sbt("xbf0", [128, TC], BF16); sbt("xbf1", [128, TC], BF16)
        sbt("tmp0", [128, TC], F32); sbt("tmp1", [128, TC], F32)
        sbt("xnw0", [128, TC], F32); sbt("xnw1", [128, TC], F32)
        sbt("tbf", [128, TC], BF16)
        sbt("natb0", [128, TC], BF16); sbt("natb1", [128, TC], BF16)
        sbt("qt0", [128, 4 * QT], BF16); sbt("qt1", [128, 4 * QT], BF16)
        for q3 in range(4):
            sbt("ktb%d" % q3, [128, 512], BF16)
            sbt("ktT%d" % q3, [128, 512], BF16)
            sbt("vtb%d" % q3, [128, 512], BF16)
        sbt("qt2", [128, 4 * QT], BF16)
        sbt("pbf0", [128, 4 * QT], BF16); sbt("pbf1", [128, 4 * QT], BF16)
        sbt("recb", [128, TC], F32)
        sbt("onesb", [128, 1], BF16)
        sbt("identb", [128, 128], BF16)
        sbt("bcsb", [128, ND], F32); sbt("b2c", [128, ND], F32)
        sbt("tcol", [128, 1], F32)
        sbt("Cb", [128, NJ * NKV], F32)
        sbt("rbsb", [8, 1], F32)
        sbt("lg", [8, 1], F32); sbt("elg", [8, 1], F32); sbt("elgbf", [8, 1], BF16)
        sbt("rec8", [8, 1], F32); sbt("rt2", [8, 1], F32); sbt("rt2bf", [8, 1], BF16)
        sbt("qsum", [128, ND], F32); sbt("qsumbf", [128, ND], BF16)
        sbt("m_row", [1, TC], F32); sbt("rstd_row", [1, TC], F32)
        sbt("ex2_row", [1, TC], F32)
        sbt("scr_row", [1, TC], F32)
        sbt("kmxr", [1, 16], F32); sbt("kmxg", [1, 16], F32)
        sbt("qmaxr", [1, 128], F32)
        sbt("c2a", [1, 32], F32); sbt("c2b", [1, 32], F32)
        sbt("cpos", [1, 32], F32); sbt("cneg", [1, 32], F32)

        pst("psA", [128, 1024], F32)
        pst("psB", [128, 1024], F32)
        pst("psC", [128, 1024], F32)
        pst("psD", [128, 512], F32)
        pst("psTR", [128, 1024], BF16)

        for n in ("cst", "sG", "sMM", "sACT", "sDVE", "sXD",
                  "sXa", "sXb", "sWa", "sWb",
                  "sKT0", "sKT1", "sKT2", "sKT3",
                  "sVT0", "sVT1", "sVT2", "sVT3",
                  "sQT0", "sQT1", "sQT2", "sMK0", "sMK1",
                  "sSMM", "sEXP", "sMSK", "sPV", "sFIN", "sTRK", "sKTT",
                  "sOEV", "sOTT",
                  "sNB0", "sNB1", "sSTn0", "sSTn1", "sSTx0", "sSTx1",
                  "cAR1", "cAR2", "cARK", "cAG"):
            mksem(n)

        EV = {k: [] for k in ("tensor", "scalar", "vector", "sync", "gpsimd")}
        FREE = {}

        def emit(q, fn):
            EV[q].append(fn)

        def wait(q, s, v):
            if v <= 0:
                return
            emit(q, lambda e, s=s, v=v: e.wait_ge(SEMH[s], v))

        def wfree(q, name):
            if name in FREE:
                s, v = FREE[name]
                wait(q, s, v)

        def op(q, fn, inc=None, n=1):
            if inc is not None:
                CNT[inc] += n
            def f(e, fn=fn, inc=inc, n=n):
                ins = fn(e)
                if inc is not None:
                    ins.then_inc(SEMH[inc], n)
            emit(q, f)
            return CNT[inc] if inc else None

        def dma(q, outap, inap, s, transpose=False, slow=False):
            CNT[s] += 16
            def f(e, o=outap, i=inap, s=s, tr=transpose, sl=slow):
                if tr:
                    e.dma_start_transpose(out=o, in_=i).then_inc(SEMH[s], 16)
                else:
                    e.dma_start(out=o, in_=i,
                                allow_slow_non_contiguous=sl).then_inc(SEMH[s], 16)
            emit(q, f)
            return CNT[s]

        X = S

        def xs(i): return X["xs%d" % (i % 2)]
        def xbf(i): return X["xbf%d" % (i % 2)]
        def xsq(i): return X["pbf%d" % (i % 2)]
        def tmp(i): return X["tmp%d" % (i % 2)]
        def xnw(i): return X["xnw%d" % (i % 2)]
        def slab(i): return X["slab%d" % (i % 2)]
        def evb(i): return X["xbf%d" % (i % 2)]
        def natb(i): return X["natb%d" % (i % 2)]
        def xsem(i): return "sX%s" % ("ab"[i % 2])
        def wsem(i): return "sW%s" % ("ab"[i % 2])
        def stx(i): return "sSTx%d" % (i % 2)

        dma("gpsimd", X["abf"][:].rearrange("p (c r) -> p c r", r=128),
            ap["aT"].rearrange("(c p) r -> p c r", p=128), "cst")
        dma("gpsimd", X["btf"][:], ap["bT"], "cst")
        dma("gpsimd", X["rwbf"][:].rearrange("p (c e) -> p c e", e=8),
            ap["rwT"].rearrange("(c p) e -> p c e", p=128), "cst")
        dma("gpsimd", X["eselbf"][:], ap["esel"], "cst")
        dma("gpsimd", X["identb"][:], ap["ident"], "cst")
        for nm in ("cosk", "sink"):
            dma("gpsimd", X["s_" + nm][:], ap[nm], "cst")
        dma("gpsimd", X["bcsb"][:], ap["bcols"], "cst")
        dma("gpsimd", X["rbsb"][:], ap["rb8"], "cst")
        dma("gpsimd", X["wb"][:], ap["wrow"][0:1, :].broadcast_to([128, TC]), "cst")
        cst_total = CNT["cst"]

        wait("vector", "cst", cst_total)
        op("vector", lambda e: e.memset(X["onesb"][:], 1.0))
        op("vector", lambda e: e.tensor_scalar_mul(X["b2c"][:], X["bcsb"][:], 0.5), "sDVE")

        def stats_mms(o, first, last):
            def f(e, o=o, first=first, last=last):
                e.matmul(PS["psC"][0:1, 0:512], X["onesb"][:], xbf(o)[:, 0:512],
                         start=first, stop=last, skip_group_check=True)
                e.matmul(PS["psC"][0:1, 512:1024], X["onesb"][:], xbf(o)[:, 512:1024],
                         start=first, stop=last, skip_group_check=True)
                e.matmul(PS["psD"][0:1, 0:512], X["onesb"][:], xsq(o)[:, 0:512],
                         start=first, stop=last, skip_group_check=True)
                return e.matmul(PS["psD"][32:33, 0:512], X["onesb"][:],
                                xsq(o)[:, 512:1024],
                                start=first, stop=last, skip_group_check=True)
            v = op("tensor", f, "sMM")
            FREE["xbq%d" % (o % 2)] = ("sMM", v)
            return v

        def ln_finalize(with_wrb):
            ps = PS["psC"]
            wait("vector", "sMM", CNT["sMM"])
            op("vector", lambda e: e.tensor_scalar_mul(X["m_row"][:], ps[0:1, :], 1.0 / D))
            op("vector", lambda e: e.tensor_scalar_mul(
                X["ex2_row"][0:1, 0:512], PS["psD"][0:1, 0:512], 1.0 / D))
            op("vector", lambda e: e.tensor_scalar_mul(
                X["ex2_row"][0:1, 512:1024], PS["psD"][32:33, 0:512], 1.0 / D))
            op("vector", lambda e: e.tensor_mul(X["scr_row"][:], X["m_row"][:], X["m_row"][:]))
            op("vector", lambda e: e.tensor_sub(X["ex2_row"][:], X["ex2_row"][:],
                                                X["scr_row"][:]))
            op("vector", lambda e: e.tensor_scalar_add(X["ex2_row"][:], X["ex2_row"][:],
                                                       EPS), "sDVE")
            FREE["psC"] = ("sDVE", CNT["sDVE"])
            wait("scalar", "sDVE", CNT["sDVE"])
            op("scalar", lambda e: e.activation(X["ex2_row"][:], X["ex2_row"][:], AF.Ln))
            op("scalar", lambda e: e.activation(X["rstd_row"][:], X["ex2_row"][:], AF.Exp,
                                                scale=-0.5), "sACT")
            wait("gpsimd", "sACT", CNT["sACT"])
            dma("gpsimd", ap["rowsd"][0:1, :], X["m_row"][:], "sG")
            dma("gpsimd", ap["rowsd"][1:2, :], X["rstd_row"][:], "sG")
            wait("gpsimd", "sG", CNT["sG"])
            dma("gpsimd", X["mb"][:], ap["rowsd"][0:1, :].broadcast_to([128, TC]), "sG")
            g_done = dma("gpsimd", X["rb_"][:],
                         ap["rowsd"][1:2, :].broadcast_to([128, TC]), "sG")
            wait("vector", "sG", g_done)
            if with_wrb:
                op("vector", lambda e: e.tensor_mul(X["wrb"][:], X["rb_"][:], X["wb"][:]),
                   "sDVE")

        def ln_apply(src_ap, with_qsum, store_counts=None):
            if store_counts is None:
                wait("sync", "sSTx0", CNT["sSTx0"])
                wait("sync", "sSTx1", CNT["sSTx1"])
            for s_ in range(ND):
                if store_counts is not None:
                    wait("sync", stx(s_), store_counts[s_])
                wfree(q="sync", name="xs%d" % (s_ % 2))
                lv = dma("sync", xs(s_)[:], src_ap[128 * s_:128 * (s_ + 1), :], xsem(s_))
                wait("vector", xsem(s_), lv)
                if with_qsum:
                    wfree("vector", "xnw%d" % (s_ % 2))
                op("vector", lambda e, s_=s_: e.tensor_sub(tmp(s_)[:], xs(s_)[:], X["mb"][:]))
                op("vector", lambda e, s_=s_: e.tensor_mul(
                    X["xln"][:, s_ * TC:(s_ + 1) * TC], tmp(s_)[:], X["rb_"][:]))
                if with_qsum:
                    op("vector", lambda e, s_=s_: e.tensor_mul(xnw(s_)[:], tmp(s_)[:],
                                                               X["wrb"][:]))
                    v = op("vector", lambda e, s_=s_: e.tensor_reduce(
                        X["qsum"][:, s_:s_ + 1], xnw(s_)[:], mybir.AxisListType.X,
                        ALU.add), "sDVE")
                else:
                    v = op("vector", lambda e: e.engine_nop(), "sDVE")
                FREE["xs%d" % (s_ % 2)] = ("sDVE", v)
            return CNT["sDVE"]

        def ffn_phase(x_src_ap, x_dst_ap, w_src, car, do_stats):
            store_counts = []
            wait("tensor", "sDVE", CNT["sDVE"])
            def tmm(e):
                ins = None
                for th in range(2):
                    for d_ in range(ND):
                        ins = e.matmul(PS["psC"][:, th * 512:(th + 1) * 512],
                                       X["abf"][:, d_ * 128:(d_ + 1) * 128],
                                       X["xln"][:, d_ * TC + th * 512: d_ * TC + th * 512 + 512],
                                       start=(d_ == 0), stop=(d_ == ND - 1),
                                       skip_group_check=True)
                return ins
            wfree("tensor", "psC")
            op("tensor", tmm, "sMM")
            mm_t = CNT["sMM"]

            aro = ap["ar1o"] if car == "cAR1" else ap["ar2o"]
            def router_chain():
                wait("gpsimd", car, 1)
                gq = dma("gpsimd", X["qsum"][:], aro[:, :], "sG")
                wait("vector", "sG", gq)
                op("vector", lambda e: e.tensor_scalar_mul(X["qsumbf"][:], X["qsum"][:],
                                                           1.0 / NQMASK), "sDVE")
                wait("tensor", "sDVE", CNT["sDVE"])
                def lmm(e):
                    ins = None
                    for d_ in range(ND):
                        ins = e.matmul(PS["psD"][0:8, 0:1],
                                       X["rwbf"][:, d_ * 8:(d_ + 1) * 8],
                                       X["qsumbf"][:, d_:d_ + 1],
                                       start=(d_ == 0), stop=(d_ == ND - 1),
                                       skip_group_check=True)
                    return ins
                op("tensor", lmm, "sMM")
                wait("vector", "sMM", CNT["sMM"])
                op("vector", lambda e: e.tensor_add(X["lg"][:], PS["psD"][0:8, 0:1],
                                                    X["rbsb"][:]), "sDVE")
                wait("scalar", "sDVE", CNT["sDVE"])
                op("scalar", lambda e: e.activation(X["elg"][:], X["lg"][:], AF.Exp), "sACT")
                wait("vector", "sACT", CNT["sACT"])
                op("vector", lambda e: e.tensor_copy(X["elgbf"][:], X["elg"][:]), "sDVE")
                wait("tensor", "sDVE", CNT["sDVE"])
                op("tensor", lambda e: e.matmul(PS["psD"][0:1, 4:5], X["onesb"][0:8, 0:1],
                                                X["elgbf"][:], start=True, stop=True,
                                                skip_group_check=True), "sMM")
                wait("vector", "sMM", CNT["sMM"])
                op("vector", lambda e: e.reciprocal(X["rec8"][0:1, :], PS["psD"][0:1, 4:5]),
                   "sDVE")
                wait("gpsimd", "sDVE", CNT["sDVE"])
                dma("gpsimd", ap["rowsd"][2:3, 0:1], X["rec8"][0:1, :], "sG")
                wait("gpsimd", "sG", CNT["sG"])
                g8 = dma("gpsimd", X["rec8"][:], ap["rowsd"][2:3, 0:1].broadcast_to([8, 1]),
                         "sG")
                wait("vector", "sG", g8)
                op("vector", lambda e: e.tensor_scalar(X["rt2bf"][:], X["elg"][:],
                                                        X["rec8"][:, 0:1], 2.0,
                                                        ALU.mult, ALU.mult), "sDVE")
                wait("tensor", "sDVE", CNT["sDVE"])
                op("tensor", lambda e: e.matmul(PS["psD"][:, 8:9], X["eselbf"][:],
                                                X["rt2bf"][:], start=True, stop=True,
                                                skip_group_check=True), "sMM")
                wait("vector", "sMM", CNT["sMM"])
                op("vector", lambda e: e.tensor_copy(X["tcol"][:], PS["psD"][:, 8:9]))
                wait("vector", "sMM", mm_t)
                v = op("vector", lambda e: e.tensor_single_scalar(
                    X["tbf"][:], PS["psC"][:, :], X["tcol"][:, 0:1], ALU.mult), "sDVE")
                FREE["psC"] = ("sDVE", v)
                return v

            act_base = CNT["sACT"]
            for o in range(ND):
                wfree("gpsimd", "slab%d" % (o % 2))
                lw = dma("gpsimd", slab(o)[:].rearrange("p (c o) -> p c o", o=128),
                         ap[w_src].rearrange("(c p) o -> p c o", p=128)
                         [:, :, 128 * o:128 * (o + 1)], wsem(o))
                wfree("sync", "xs%d" % (o % 2))
                lx = dma("sync", xs(o)[:], x_src_ap[128 * o:128 * (o + 1), :], xsem(o))
                wait("tensor", wsem(o), lw)
                if o >= 2:
                    wait("tensor", "sACT", act_base + o)
                psn = "psA" if o % 2 == 0 else "psB"
                def base_only(e, o=o, psn=psn):
                    ins = None
                    for th in range(2):
                        for d_ in range(ND):
                            ins = e.matmul(PS[psn][:, th * 512:(th + 1) * 512],
                                     slab(o)[:, d_ * 128:(d_ + 1) * 128],
                                     X["xln"][:, d_ * TC + th * 512: d_ * TC + th * 512 + 512],
                                     start=(d_ == 0), stop=False, skip_group_check=True)
                    return ins
                def lora_only(e, o=o, psn=psn):
                    ins = None
                    for th in range(2):
                        ins = e.matmul(PS[psn][:, th * 512:(th + 1) * 512],
                                       X["btf"][:, 128 * o:128 * (o + 1)],
                                       X["tbf"][:, th * 512:(th + 1) * 512],
                                       start=False, stop=True, skip_group_check=True)
                    return ins
                def bmm(e, bo=base_only, lo=lora_only):
                    bo(e)
                    return lo(e)
                if o == 0:
                    op("tensor", base_only)
                    dve_t = router_chain()
                    wait("tensor", "sDVE", dve_t)
                    v = op("tensor", lora_only, "sMM")
                else:
                    v = op("tensor", bmm, "sMM")
                FREE["slab%d" % (o % 2)] = ("sMM", v)
                mm_o = v
                wait("scalar", "sMM", mm_o)
                wfree("scalar", "tmp%d" % (o % 2))
                va = op("scalar", lambda e, o=o, psn=psn: e.activation(
                    tmp(o)[:], PS[psn][:, :], AF.Identity,
                    bias=X["b2c"][:, o:o + 1], scale=0.5), "sACT")
                wait("vector", "sACT", va)
                wait("vector", xsem(o), lx)
                wfree("vector", "xnw%d" % (o % 2))
                vx = op("vector", lambda e, o=o: e.tensor_add(xnw(o)[:], xs(o)[:],
                                                              tmp(o)[:]), "sXD")
                FREE["xs%d" % (o % 2)] = ("sXD", vx)
                FREE["tmp%d" % (o % 2)] = ("sXD", vx)
                wait("sync", "sXD", vx)
                vst = dma("sync", x_dst_ap[128 * o:128 * (o + 1), :], xnw(o)[:], stx(o))
                FREE["xnw%d" % (o % 2)] = (stx(o), vst)
                store_counts.append(vst)
                if do_stats:
                    wfree("vector", "xbq%d" % (o % 2))
                    op("vector", lambda e, o=o: e.tensor_copy(xbf(o)[:], xnw(o)[:]))
                    op("vector", lambda e, o=o: e.tensor_mul(xsq(o)[:], xbf(o)[:],
                                                             xbf(o)[:]), "sDVE")
                    wait("tensor", "sDVE", CNT["sDVE"])
                    stats_mms(o, o == 0, o == ND - 1)
            return store_counts

        wait("tensor", "cst", cst_total)
        for s_ in range(ND):
            wfree("sync", "xs%d" % (s_ % 2))
            lv = dma("sync", xs(s_)[:], ap["xT"][128 * s_:128 * (s_ + 1), :], xsem(s_))
            wait("vector", xsem(s_), lv)
            wfree("vector", "xbq%d" % (s_ % 2))
            op("vector", lambda e, s_=s_: e.tensor_copy(xbf(s_)[:], xs(s_)[:]))
            v = op("vector", lambda e, s_=s_: e.tensor_mul(xsq(s_)[:], xbf(s_)[:],
                                                           xbf(s_)[:]), "sDVE")
            FREE["xs%d" % (s_ % 2)] = ("sDVE", v)
            wait("tensor", "sDVE", v)
            stats_mms(s_, s_ == 0, s_ == ND - 1)
        ln_finalize(True)
        ln_apply(ap["xT"], True)
        wait("gpsimd", "sDVE", CNT["sDVE"])
        dma("gpsimd", ap["ar1i"][:, :], X["qsum"][:], "sG")
        wait("gpsimd", "sG", CNT["sG"])
        op("gpsimd", lambda e: e.collective_compute(
            "AllReduce", ALU.add, replica_groups=[list(range(NCORE))],
            ins=[ap["ar1i"].opt()], outs=[ap["ar1o"].opt()]), "cAR1")

        st1 = None
        if ph_max >= 1:
            st1 = ffn_phase(ap["xT"], ap["x1d"], "wfT", "cAR1", True)

        if ph_max >= 2:
            ln_finalize(False)
            ln_apply(ap["x1d"], False, st1)
            dve_ln2 = CNT["sDVE"]

            for sec in range(2):
                wsrc = "wkT" if sec == 0 else "wvT"
                for g in range(NKV):
                    i = sec * NKV + g
                    wfree("gpsimd", "slab%d" % (i % 2))
                    lw = dma("gpsimd", slab(i)[:].rearrange("p (c o) -> p c o", o=128),
                             ap[wsrc].rearrange("(c p) o -> p c o", p=128)
                             [:, :, 128 * g:128 * (g + 1)], wsem(i))
                    wait("tensor", wsem(i), lw)
                    if i == 0:
                        wait("tensor", "sDVE", dve_ln2)
                    psn = "psA" if i % 2 == 0 else "psB"
                    wfree("tensor", psn)
                    def pmm(e, i=i, psn=psn):
                        ins = None
                        for th in range(2):
                            for d_ in range(ND):
                                ins = e.matmul(PS[psn][:, th * 512:(th + 1) * 512],
                                               slab(i)[:, d_ * 128:(d_ + 1) * 128],
                                               X["xln"][:, d_ * TC + th * 512: d_ * TC + th * 512 + 512],
                                               start=(d_ == 0), stop=(d_ == ND - 1),
                                               skip_group_check=True)
                        return ins
                    v = op("tensor", pmm, "sMM")
                    FREE["slab%d" % (i % 2)] = ("sMM", v)
                    mm_p = v
                    if sec == 0:
                        wait("vector", "sMM", mm_p)
                        wfree("vector", "evb%d" % (g % 2))
                        wfree("vector", "xbq%d" % (g % 2))
                        op("vector", lambda e, psn=psn: e.tensor_mul(
                            tmp(0)[:], PS[psn][:, :], X["s_cosk"][:]))
                        op("vector", lambda e, psn=psn: e.tensor_mul(
                            tmp(1)[0:64, :], PS[psn][64:128, :], X["s_sink"][0:64, :]))
                        op("vector", lambda e, psn=psn: e.tensor_mul(
                            tmp(1)[64:128, :], PS[psn][0:64, :], X["s_sink"][64:128, :]))
                        op("vector", lambda e, g=g: e.tensor_add(
                            evb(g)[:], tmp(0)[:], tmp(1)[:]))
                        wfree("vector", "xbq%d" % (g % 2))
                        v2 = op("vector", lambda e, g=g: e.tensor_mul(
                            xsq(g)[:], evb(g)[:], evb(g)[:]), "sDVE")
                        FREE[psn] = ("sDVE", v2)
                        wait("tensor", "sDVE", v2)
                        def knmm(e, g=g):
                            e.matmul(PS["psC"][0:1, 0:512], X["onesb"][:],
                                     xsq(g)[:, 0:512], start=True, stop=True,
                                     skip_group_check=True)
                            return e.matmul(PS["psC"][0:1, 512:1024], X["onesb"][:],
                                            xsq(g)[:, 512:1024], start=True, stop=True,
                                            skip_group_check=True)
                        vkn = op("tensor", knmm, "sMM")
                        FREE["xbq%d" % (g % 2)] = ("sMM", vkn)
                        wait("vector", "sMM", vkn)
                        op("vector", lambda e, g=g: e.tensor_reduce(
                            X["kmxr"][0:1, g:g + 1], PS["psC"][0:1, :],
                            mybir.AxisListType.X, ALU.max), "sDVE")
                    else:
                        wait("scalar", "sMM", mm_p)
                        wfree("scalar", "evb%d" % (g % 2))
                        va = op("scalar", lambda e, g=g, psn=psn: e.activation(
                            evb(g)[:], PS[psn][:, :], AF.Identity), "sACT")
                        FREE[psn] = ("sACT", va)
                        wait("tensor", "sACT", va)
                    def trs(e, g=g):
                        ins = None
                        for u in range(8):
                            ins = e.transpose(PS["psTR"][:, u * 128:(u + 1) * 128],
                                              evb(g)[:, u * 128:(u + 1) * 128],
                                              X["identb"][:])
                        return ins
                    vtr = op("tensor", trs, "sMM")
                    FREE["evb%d" % (g % 2)] = ("sMM", vtr)
                    wait("vector", "sMM", vtr)
                    wfree("vector", "natb%d" % (g % 2))
                    vnb = op("vector", lambda e, g=g: e.tensor_copy(natb(g)[:],
                                                                    PS["psTR"][:, :]),
                             "sDVE")
                    wait("sync", "sDVE", vnb)
                    vst = dma("sync",
                              ap["kvin"][(sec * NKV + g) * TC:(sec * NKV + g + 1) * TC, :]
                              .rearrange("(s p) h -> p s h", p=128),
                              natb(g)[:].rearrange("p (s h) -> p s h", h=128),
                              "sNB%d" % (g % 2))
                    FREE["natb%d" % (g % 2)] = ("sNB%d" % (g % 2), vst)
            wq_pre = {}
            for h0 in range(2):
                i0 = 2 * NKV + h0
                wfree("gpsimd", "slab%d" % (i0 % 2))
                wq_pre[h0] = dma("gpsimd",
                                 slab(i0)[:].rearrange("p (c o) -> p c o", o=128),
                                 ap["wqT"].rearrange("(c p) o -> p c o", p=128)
                                 [:, :, 128 * h0:128 * (h0 + 1)], wsem(i0))
            wait("gpsimd", "sNB0", CNT["sNB0"])
            wait("gpsimd", "sNB1", CNT["sNB1"])
            op("gpsimd", lambda e: e.collective_compute(
                "AllGather", ALU.bypass, replica_groups=[list(range(NCORE))],
                ins=[ap["kvin"].opt()], outs=[ap["kvout"].opt()]), "cAG")
            wait("gpsimd", "sDVE", CNT["sDVE"])
            dma("gpsimd", ap["arki"][0:1, :], X["kmxr"][0:1, 0:8], "sG")
            wait("gpsimd", "sG", CNT["sG"])
            op("gpsimd", lambda e: e.collective_compute(
                "AllReduce", ALU.max, replica_groups=[list(range(NCORE))],
                ins=[ap["arki"].opt()], outs=[ap["arko"].opt()]), "cARK")

            kvo2 = ap["kvout"].rearrange("(r blk row) h -> r blk row h",
                                         blk=2 * NKV, row=TC)
            ktb_built = {"n": 0}
            def build_kT_super(sg):
                g_ = sg // 8
                u_ = sg % 8
                for half in range(2):
                    uu = u_ * 2 + half
                    sl = (2 * sg + half) % 4
                    wfree("sync", "ktb%d" % sl)
                    ktv = None
                    for r2_ in range(2):
                        ktv = dma("sync",
                                  X["ktb%d" % sl][64 * r2_:64 * (r2_ + 1), :]
                                  .rearrange("p (kc h) -> p kc h", h=128),
                                  kvo2[r2_::2, g_, 64 * uu:64 * uu + 64, :]
                                  .rearrange("kc u_ h -> u_ kc h"),
                                  "sKT%d" % sl)
                    wait("tensor", "sKT%d" % sl, ktv)
                    if ktb_built["n"] > 0:
                        wait("tensor", "sKTT", CNT["sKTT"])
                    def ktr(e, sl=sl):
                        ins = None
                        for kc_ in range(4):
                            ins = e.transpose(
                                PS["psTR"][:, kc_ * 128:(kc_ + 1) * 128],
                                X["ktb%d" % sl][:, kc_ * 128:(kc_ + 1) * 128],
                                X["identb"][:])
                        return ins
                    vtr = op("tensor", ktr, "sTRK")
                    FREE["ktb%d" % sl] = ("sTRK", vtr)
                    wait("vector", "sTRK", vtr)
                    wfree("vector", "ktT%d" % sl)
                    vc = op("vector", lambda e, sl=sl: e.tensor_copy(
                        X["ktT%d" % sl][:], PS["psTR"][:, 0:512]), "sKTT")
                    wait("sync", "sKTT", vc)
                    vst = dma("sync", ap["kTd"][g_, :, 512 * uu:512 * (uu + 1)],
                              X["ktT%d" % sl][:], "sSTn0")
                    FREE["ktT%d" % sl] = ("sSTn0", vst)
                    ktb_built["n"] += 1

            for h in range(NH):
                i = 2 * NKV + h
                if h == 0:
                    wait("sync", "cAG", 1)
                    wait("tensor", "cAG", 1)
                if h < 2:
                    lw = wq_pre[h]
                else:
                    wfree("gpsimd", "slab%d" % (i % 2))
                    lw = dma("gpsimd", slab(i)[:].rearrange("p (c o) -> p c o", o=128),
                             ap["wqT"].rearrange("(c p) o -> p c o", p=128)
                             [:, :, 128 * h:128 * (h + 1)], wsem(i))
                wait("tensor", wsem(i), lw)
                psn = "psA" if i % 2 == 0 else "psB"
                wfree("tensor", psn)
                def qmm(e, i=i, psn=psn):
                    ins = None
                    for th in range(2):
                        for d_ in range(ND):
                            ins = e.matmul(PS[psn][:, th * 512:(th + 1) * 512],
                                           slab(i)[:, d_ * 128:(d_ + 1) * 128],
                                           X["xln"][:, d_ * TC + th * 512: d_ * TC + th * 512 + 512],
                                           start=(d_ == 0), stop=(d_ == ND - 1),
                                           skip_group_check=True)
                    return ins
                v = op("tensor", qmm, "sMM")
                FREE["slab%d" % (i % 2)] = ("sMM", v)
                wait("vector", "sMM", v)
                wfree("vector", "evb%d" % (h % 2))
                wfree("vector", "xbq%d" % (h % 2))
                op("vector", lambda e, psn=psn: e.tensor_mul(
                    tmp(0)[:], PS[psn][:, :], X["s_cosk"][:]))
                op("vector", lambda e, psn=psn: e.tensor_mul(
                    tmp(1)[0:64, :], PS[psn][64:128, :], X["s_sink"][0:64, :]))
                op("vector", lambda e, psn=psn: e.tensor_mul(
                    tmp(1)[64:128, :], PS[psn][0:64, :], X["s_sink"][64:128, :]))
                op("vector", lambda e, h=h: e.tensor_add(
                    evb(h)[:], tmp(0)[:], tmp(1)[:]))
                wfree("vector", "xbq%d" % (h % 2))
                vq = op("vector", lambda e, h=h: e.tensor_mul(
                    xsq(h)[:], evb(h)[:], evb(h)[:]), "sDVE")
                FREE[psn] = ("sDVE", vq)
                wait("sync", "sDVE", vq)
                vst = dma("sync", ap["qd"][128 * h:128 * (h + 1), :], evb(h)[:],
                          "sSTn%d" % (h % 2))
                FREE["evb%d" % (h % 2)] = ("sSTn%d" % (h % 2), vst)
                wait("tensor", "sDVE", vq)
                def qnmm(e, h=h):
                    e.matmul(PS["psC"][32:33, 0:512], X["onesb"][:],
                             xsq(h)[:, 0:512], start=True, stop=True,
                             skip_group_check=True)
                    return e.matmul(PS["psC"][32:33, 512:1024], X["onesb"][:],
                                    xsq(h)[:, 512:1024], start=True, stop=True,
                                    skip_group_check=True)
                vqn = op("tensor", qnmm, "sMM")
                FREE["xbq%d" % (h % 2)] = ("sMM", vqn)
                wait("vector", "sMM", vqn)
                for j in range(NJ):
                    op("vector", lambda e, h=h, j=j: e.tensor_reduce(
                        X["qmaxr"][0:1, h * 4 + j:h * 4 + j + 1],
                        PS["psC"][32:33, j * QT:(j + 1) * QT],
                        mybir.AxisListType.X, ALU.max),
                        "sDVE" if j == NJ - 1 else None)
                build_kT_super(2 * h)
                build_kT_super(2 * h + 1)

        if ph_max >= 3:
            wait("gpsimd", "cARK", 1)
            gk = dma("gpsimd", X["kmxg"][0:1, 0:8], ap["arko"][0:1, :], "sG")
            wait("vector", "sG", gk)
            wait("vector", "sDVE", CNT["sDVE"])
            for g in range(NKV):
                for j in range(NJ):
                    idx = g * NJ + j
                    op("vector", lambda e, g=g, j=j, idx=idx: e.tensor_reduce(
                        X["c2a"][0:1, idx:idx + 1],
                        X["qmaxr"][0:1, :].rearrange("a (h b) -> a b h", b=4)
                        [:, j:j + 1, 4 * g:4 * g + 4],
                        mybir.AxisListType.X, ALU.max))
                    op("vector", lambda e, g=g, idx=idx: e.tensor_single_scalar(
                        X["c2b"][0:1, idx:idx + 1], X["c2a"][0:1, idx:idx + 1],
                        X["kmxg"][0:1, g:g + 1], ALU.mult),
                        "sDVE" if idx == NKV * NJ - 1 else None)
            wait("scalar", "sDVE", CNT["sDVE"])
            op("scalar", lambda e: e.activation(X["cpos"][:], X["c2b"][:], AF.Sqrt),
               "sACT")
            wait("vector", "sACT", CNT["sACT"])
            op("vector", lambda e: e.tensor_scalar_mul(X["cneg"][:], X["cpos"][:], -0.08838834764831845),
               "sDVE")
            wait("gpsimd", "sDVE", CNT["sDVE"])
            dma("gpsimd", ap["rowsd"][3:4, 0:32], X["cneg"][:], "sG")
            wait("gpsimd", "sG", CNT["sG"])
            gcb = dma("gpsimd", X["Cb"][:],
                      ap["rowsd"][3:4, 0:32].broadcast_to([128, 32]), "sG")
            wait("scalar", "sG", gcb)

            wait("sync", "cAG", 1)
            wait("sync", "sSTn0", CNT["sSTn0"])
            wait("sync", "sSTn1", CNT["sSTn1"])
            kvo = ap["kvout"].rearrange("(r blk row) h -> r blk row h",
                                        blk=2 * NKV, row=TC)

            gM = 0
            exp_of = {}
            pready = {}
            pv_of = {}
            smm_of = {}
            kload = {}
            copy_of = {}
            sup_desc = []
            for j_ in range(NJ):
                for g_ in range(NKV):
                    for u_ in range(4 * (j_ + 1)):
                        sup_desc.append((j_, g_, u_))
            NSUP = len(sup_desc)

            def emit_super_load(sg):
                j_, g_, u_ = sup_desc[sg]
                sl = sg % 4
                wfree("sync", "ktT%d" % sl)
                ktv = dma("sync", X["ktT%d" % sl][:],
                          ap["kTd"][g_, :, 512 * u_:512 * (u_ + 1)],
                          "sKT%d" % sl)
                vtv = None
                wfree("sync", "vtb%d" % sl)
                for r2_ in range(2):
                    vtv = dma("sync",
                              X["vtb%d" % sl][64 * r2_:64 * (r2_ + 1), :]
                              .rearrange("p (kc h) -> p kc h", h=128),
                              kvo[r2_::2, NKV + g_, 64 * u_:64 * u_ + 64, :]
                              .rearrange("kc u_ h -> u_ kc h"),
                              "sVT%d" % sl)
                kload[sg] = (sl, ktv, vtv)

            emit_super_load(0)
            gsup = 0
            for j in range(NJ):
                wfree("sync", "maskb")
                mload = dma("sync", X["maskb"][:], ap["masksd"][j], "sMK0")
                for g in range(NKV):
                    gi = j * NKV + g
                    CL = 16 * (j + 1)
                    wfree("sync", "qt%d" % (gi % 3))
                    lq = dma("sync", X["qt%d" % (gi % 3)][:].rearrange(
                        "p (b c) -> p b c", c=QT),
                        ap["qd"].rearrange("(b p) t -> p b t", p=128)
                        [:, 4 * g:4 * g + 4, QT * j:QT * (j + 1)],
                        "sQT%d" % (gi % 3))
                    first_chunk = gM

                    def emit_pv(M, cid, g=g, CL=CL):
                        sslot = kload[cid // 4][0]
                        psem, pval = pready[cid]
                        wait("tensor", psem, pval)
                        if M == 0:
                            wait("tensor", "sFIN", CNT["sFIN"])
                            wait("tensor", "sOEV", CNT["sOEV"])
                        def pvm(e, cid=cid, kc=cid % 4, M=M, CL=CL, sslot=sslot):
                            for b2 in range(2):
                                e.matmul(PS["psC"][:, b2 * 512:(b2 + 1) * 512],
                                         X["vtb%d" % sslot][:, kc * 128:(kc + 1) * 128],
                                         X["pbf%d" % (cid % 2)][:, b2 * 512:(b2 + 1) * 512],
                                         start=(M == 0), stop=(M == CL - 1),
                                         skip_group_check=True)
                            e.matmul(PS["psD"][0:1, 0:512], X["onesb"][:],
                                     X["pbf%d" % (cid % 2)][:, 0:512],
                                     start=(M == 0), stop=(M == CL - 1),
                                     skip_group_check=True)
                            return e.matmul(PS["psD"][32:33, 0:512], X["onesb"][:],
                                            X["pbf%d" % (cid % 2)][:, 512:1024],
                                            start=(M == 0), stop=(M == CL - 1),
                                            skip_group_check=True)
                        v = op("tensor", pvm, "sPV")
                        pv_of[cid] = v
                        if cid % 4 == 3:
                            FREE["vtb%d" % (kload[cid // 4][0])] = ("sPV", v)

                    for M in range(CL):
                        kc = M % 4
                        if kc == 0:
                            su = gsup
                            if su + 1 < NSUP:
                                emit_super_load(su + 1)
                            gsup += 1
                        sps = "psA" if gM % 2 == 0 else "psB"
                        if gM < 2:
                            wfree("tensor", sps)
                        if kc == 0:
                            wait("tensor", "sKT%d" % (kload[su][0]), kload[su][1])
                            wait("tensor", "sVT%d" % (kload[su][0]), kload[su][2])
                        if M == 0:
                            wait("tensor", "sQT%d" % (gi % 3), lq)
                        if gM >= 2:
                            wait("tensor", "sEXP", exp_of[gM - 2])
                        def smm(e, sps=sps, sslot=kload[su][0], kc=kc, gi=gi):
                            ins = None
                            for b2 in range(2):
                                ins = e.matmul(
                                    PS[sps][:, b2 * 512:(b2 + 1) * 512],
                                    X["ktT%d" % sslot][:, kc * 128:(kc + 1) * 128],
                                    X["qt%d" % (gi % 3)][:, b2 * 512:(b2 + 1) * 512],
                                    start=True, stop=True, skip_group_check=True)
                            return ins
                        vs = op("tensor", smm, "sSMM")
                        smm_of[gM] = vs
                        if kc == 3:
                            FREE["ktT%d" % (kload[su][0])] = ("sSMM", vs)
                        wait("scalar", "sSMM", vs)
                        if gM >= 2:
                            wait("scalar", "sPV", pv_of[gM - 2])
                        ve = op("scalar", lambda e, sps=sps, gM=gM, g=g, j=j:
                                e.activation(X["pbf%d" % (gM % 2)][:], PS[sps][:, :],
                                             AF.Exp, scale=0.08838834764831845,
                                             bias=X["Cb"][:, g * NJ + j:g * NJ + j + 1]),
                                "sEXP")
                        exp_of[gM] = ve
                        if M >= CL - 16:
                            moff = (M - (CL - 16)) * QT
                            wait("vector", "sEXP", ve)
                            if M == CL - 16 and g == 0:
                                wait("vector", "sMK0", mload)
                            def msk(e, gM=gM, moff=moff):
                                ins = None
                                for b in range(4):
                                    ins = e.tensor_mul(
                                        X["pbf%d" % (gM % 2)][:, b * QT:(b + 1) * QT],
                                        X["pbf%d" % (gM % 2)][:, b * QT:(b + 1) * QT],
                                        X["maskb"][:, moff:moff + QT])
                                return ins
                            vm = op("vector", msk, "sMSK")
                            pready[gM] = ("sMSK", vm)
                        else:
                            pready[gM] = ("sEXP", ve)
                        if M >= 1:
                            emit_pv(M - 1, gM - 1)
                        gM += 1
                    emit_pv(CL - 1, gM - 1)
                    FREE["qt%d" % (gi % 3)] = ("sSMM", smm_of[gM - 1])
                    if g == NKV - 1:
                        FREE["maskb"] = ("sMSK", CNT["sMSK"])
                    wait("scalar", "sPV", pv_of[gM - 1])
                    if gi >= 1:
                        wait("scalar", "sOTT", gi)
                    vo = op("scalar", lambda e: e.activation(
                        X["tbf"][:], PS["psC"][:, :], AF.Identity), "sOEV")
                    wait("vector", "sPV", pv_of[gM - 1])
                    op("vector", lambda e: e.reciprocal(X["scr_row"][0:1, 0:512],
                                                        PS["psD"][0:1, 0:512]))
                    op("vector", lambda e: e.reciprocal(X["scr_row"][0:1, 512:1024],
                                                        PS["psD"][32:33, 0:512]),
                       "sFIN")
                    wait("gpsimd", "sFIN", CNT["sFIN"])
                    dma("gpsimd", ap["rowsd"][2:3, :], X["scr_row"][:], "sG")
                    wait("gpsimd", "sG", CNT["sG"])
                    grb = dma("gpsimd", X["recb"][:],
                              ap["rowsd"][2:3, :].broadcast_to([128, TC]), "sG")
                    wait("vector", "sG", grb)
                    wait("vector", "sOEV", vo)
                    op("vector", lambda e, g=g, j=j: e.tensor_mul(
                        X["xln"][:].rearrange("p (hh t) -> p hh t", t=TC)
                        [:, 4 * g:4 * g + 4, QT * j:QT * (j + 1)],
                        X["tbf"][:].rearrange("p (h c) -> p h c", c=QT),
                        X["recb"][:].rearrange("p (h c) -> p h c", c=QT)), "sOTT")

            st2 = []
            fin_all = CNT["sOTT"]
            for o in range(ND):
                wfree("gpsimd", "slab%d" % (o % 2))
                lw = dma("gpsimd", slab(o)[:].rearrange("p (c o) -> p c o", o=128),
                         ap["woT"].rearrange("(c p) o -> p c o", p=128)
                         [:, :, 128 * o:128 * (o + 1)], wsem(o))
                if o == 0:
                    wait("sync", "sSTx0", CNT["sSTx0"])
                    wait("sync", "sSTx1", CNT["sSTx1"])
                wfree("sync", "xs%d" % (o % 2))
                lx = dma("sync", xs(o)[:], ap["x1d"][128 * o:128 * (o + 1), :], xsem(o))
                wait("tensor", wsem(o), lw)
                if o == 0:
                    wait("tensor", "sOTT", fin_all)
                psn = "psA" if o % 2 == 0 else "psB"
                wfree("tensor", psn)
                def wmm(e, o=o, psn=psn):
                    ins = None
                    for th in range(2):
                        for d_ in range(ND):
                            ins = e.matmul(PS[psn][:, th * 512:(th + 1) * 512],
                                           slab(o)[:, d_ * 128:(d_ + 1) * 128],
                                           X["xln"][:, d_ * TC + th * 512: d_ * TC + th * 512 + 512],
                                           start=(d_ == 0), stop=(d_ == ND - 1),
                                           skip_group_check=True)
                    return ins
                v = op("tensor", wmm, "sMM")
                FREE["slab%d" % (o % 2)] = ("sMM", v)
                wait("vector", "sMM", v)
                wait("vector", xsem(o), lx)
                wfree("vector", "xnw%d" % (o % 2))
                vx = op("vector", lambda e, o=o, psn=psn: e.tensor_add(
                    xnw(o)[:], PS[psn][:, :], xs(o)[:]), "sXD")
                FREE[psn] = ("sXD", vx)
                FREE["xs%d" % (o % 2)] = ("sXD", vx)
                wait("sync", "sXD", vx)
                vst = dma("sync", ap["x2d"][128 * o:128 * (o + 1), :], xnw(o)[:], stx(o))
                FREE["xnw%d" % (o % 2)] = (stx(o), vst)
                st2.append(vst)
                wfree("vector", "xbq%d" % (o % 2))
                op("vector", lambda e, o=o: e.tensor_copy(xbf(o)[:], xnw(o)[:]))
                op("vector", lambda e, o=o: e.tensor_mul(xsq(o)[:], xbf(o)[:], xbf(o)[:]),
                   "sDVE")
                wait("tensor", "sDVE", CNT["sDVE"])
                stats_mms(o, o == 0, o == ND - 1)

        if ph_max >= 4:
            ln_finalize(True)
            ln_apply(ap["x2d"], True, st2)
            wait("gpsimd", "sDVE", CNT["sDVE"])
            dma("gpsimd", ap["ar2i"][:, :], X["qsum"][:], "sG")
            wait("gpsimd", "sG", CNT["sG"])
            op("gpsimd", lambda e: e.collective_compute(
                "AllReduce", ALU.add, replica_groups=[list(range(NCORE))],
                ins=[ap["ar2i"].opt()], outs=[ap["ar2o"].opt()]), "cAR2")
            ffn_phase(ap["x2d"], ap["outT"], "wfT", "cAR2", False)

        if os.environ.get("MK_DEBUG2") and ph_max >= 2:
            dout("dbg_qd", [NH * HD, TC], BF16)
            dout("dbg_kvin", [2 * NKV * TC, HD], BF16)
            dout("dbg_kmx", [1, 16])
            dout("dbg_qmax", [1, 128])
            dout("dbg_cb", [128, NJ * NKV])
            wait("gpsimd", "sSTn0", CNT["sSTn0"])
            wait("gpsimd", "sSTn1", CNT["sSTn1"])
            wait("gpsimd", "sNB0", CNT["sNB0"])
            wait("gpsimd", "sNB1", CNT["sNB1"])
            wait("gpsimd", "sDVE", CNT["sDVE"])
            dma("gpsimd", ap["dbg_qd"], ap["qd"], "sG")
            dma("gpsimd", ap["dbg_kvin"], ap["kvin"], "sG")
            dma("gpsimd", ap["dbg_kmx"], X["kmxg"][:] if ph_max >= 3 else X["kmxr"][:], "sG")
            dma("gpsimd", ap["dbg_qmax"], X["qmaxr"][:], "sG")
            if ph_max >= 3:
                dma("gpsimd", ap["dbg_cb"], X["Cb"][:], "sG")

        if os.environ.get("MK_DEBUG3") and ph_max >= 3:
            dout("dbg_obuf", [128, ND * TC], BF16)
            wait("gpsimd", "sOTT", CNT["sOTT"])
            dma("gpsimd", ap["dbg_obuf"], X["xln"][:], "sG")

        if os.environ.get("MK_DEBUG"):
            dbg = {}
            dout("dbg_rows", [4, TC]); dout("dbg_xln", [128, TC], BF16)
            dout("dbg_cols", [128, 8]); dout("dbg_tbf", [128, TC], BF16)
            dout("dbg_small", [8, 4])
            wait("gpsimd", "sDVE", CNT["sDVE"])
            wait("gpsimd", "sMM", CNT["sMM"])
            wait("gpsimd", "sACT", CNT["sACT"])
            dma("gpsimd", ap["dbg_rows"][0:1, :], X["m_row"][:], "sG")
            dma("gpsimd", ap["dbg_rows"][1:2, :], X["rstd_row"][:], "sG")
            dma("gpsimd", ap["dbg_rows"][2:3, :], X["ex2_row"][:], "sG")
            dma("gpsimd", ap["dbg_rows"][3:4, :], X["scr_row"][:], "sG")
            dma("gpsimd", ap["dbg_xln"], X["xln"][:, 0:TC], "sG")
            dma("gpsimd", ap["dbg_cols"][:, 0:1], X["tcol"][:], "sG", slow=True)
            dma("gpsimd", ap["dbg_cols"][:, 1:2], X["b2c"][:, 0:1], "sG", slow=True)
            dma("gpsimd", ap["dbg_cols"][:, 2:3], X["wb"][:, 0:1], "sG", slow=True)
            dma("gpsimd", ap["dbg_cols"][:, 3:4], X["wrb"][:, 0:1], "sG", slow=True)
            dma("gpsimd", ap["dbg_cols"][:, 4:5], X["qsum"][:, 0:1], "sG", slow=True)
            dma("gpsimd", ap["dbg_tbf"], X["tbf"][:], "sG")
            dma("gpsimd", ap["dbg_small"][:, 0:1], X["lg"][:], "sG", slow=True)
            dma("gpsimd", ap["dbg_small"][:, 1:2], X["elg"][:], "sG", slow=True)
            dma("gpsimd", ap["dbg_small"][:, 2:3], X["rt2"][:], "sG", slow=True)
            dma("gpsimd", ap["dbg_small"][:, 3:4], X["rec8"][:], "sG", slow=True)

        if ph_max < 4:
            src = ap["x1d"] if ph_max < 3 else ap["x2d"]
            wait("gpsimd", "sSTx0", CNT["sSTx0"])
            wait("gpsimd", "sSTx1", CNT["sSTx1"])
            wait("gpsimd", "sDVE", CNT["sDVE"])
            wait("gpsimd", "sMM", CNT["sMM"])
            dma("gpsimd", ap["outT"], src, "sG")

        wait("sync", "sSTx0", CNT["sSTx0"])
        wait("sync", "sSTx1", CNT["sSTx1"])
        wait("gpsimd", "sG", CNT["sG"])

        with nc.Block() as block:
            @block.tensor
            def _(t):
                for f in EV["tensor"]:
                    f(t)

            @block.scalar
            def _(s_):
                for f in EV["scalar"]:
                    f(s_)

            @block.vector
            def _(v):
                for f in EV["vector"]:
                    f(v)

            @block.sync
            def _(sy):
                for f in EV["sync"]:
                    f(sy)

            @block.gpsimd
            def _(g):
                for f in EV["gpsimd"]:
                    f(g)

        if os.environ.get("MK_VERBOSE"):
            print("[mk] events:", {k: len(v) for k, v in EV.items()})
    return nc


def _llama3_inv_freq():
    ROPE_THETA, ROPE_FACTOR, HIGH_FF, LOW_FF, OMP = 500000.0, 8.0, 4.0, 1.0, 8192
    inv = 1.0 / ROPE_THETA ** (np.arange(0, HD, 2, dtype=np.float32) / HD)
    wavelen = 2.0 * np.pi / inv
    low_wl = OMP / LOW_FF
    high_wl = OMP / HIGH_FF
    inv_scaled = np.where(wavelen > low_wl, inv / ROPE_FACTOR, inv)
    smooth = (OMP / wavelen - LOW_FF) / (HIGH_FF - LOW_FF)
    smoothed = (1.0 - smooth) / ROPE_FACTOR * inv + smooth * inv
    is_medium = (wavelen <= low_wl) & (wavelen >= high_wl)
    return np.where(is_medium, smoothed, inv_scaled).astype(np.float32)


def _own_idx(c):
    bo = np.arange(16)
    u = np.arange(64)
    return (64 * (c + 8 * bo[:, None]) + u[None, :]).reshape(-1)


def kernel(L0_attn_x, L0_attn_Wq, L0_attn_Wk, L0_attn_Wv, L0_attn_Wo,
           L0_ffn_W, L0_ffn_b, L0_ffn_A, L0_ffn_B,
           L0_ffn_router_W, L0_ffn_router_b):
    f32 = np.float32
    x = np.ascontiguousarray(np.asarray(L0_attn_x, dtype=f32)[0])
    shared = dict(
        wfT=np.ascontiguousarray(np.asarray(L0_ffn_W, f32).T),
        wqT=np.ascontiguousarray(np.asarray(L0_attn_Wq, f32).T),
        wkT=np.ascontiguousarray(np.asarray(L0_attn_Wk, f32).T),
        wvT=np.ascontiguousarray(np.asarray(L0_attn_Wv, f32).T),
        woT=np.ascontiguousarray(np.asarray(L0_attn_Wo, f32).T),
        bcols=np.ascontiguousarray(np.asarray(L0_ffn_b, f32).reshape(ND, 128).T),
        aT=np.ascontiguousarray(np.asarray(L0_ffn_A, f32).reshape(128, D).T),
        bT=np.ascontiguousarray(
            np.asarray(L0_ffn_B, f32).transpose(0, 2, 1).reshape(128, D)),
        rwT=np.ascontiguousarray(np.asarray(L0_ffn_router_W, f32).T),
        rb8=np.ascontiguousarray(np.asarray(L0_ffn_router_b, f32).reshape(8, 1)),
        esel=np.kron(np.eye(8, dtype=f32), np.ones((1, 16), f32)),
        ident=np.eye(128, dtype=f32),
    )
    invf = _llama3_inv_freq()
    scale = f32(1.0 / math.sqrt(HD))
    in_maps = []
    for c in range(NCORE):
        idx = _own_idx(c)
        pos = idx.astype(np.float32)
        fr = pos[None, :] * invf[:, None]
        cos = np.concatenate([np.cos(fr), np.cos(fr)], axis=0).astype(f32)
        sin = np.sin(fr).astype(f32)
        sin_rot = np.concatenate([-sin, sin], axis=0).astype(f32)
        wrowv = ((idx >= 611) & (idx <= 8190)).astype(f32).reshape(1, TC)
        masks = np.zeros((NJ, 128, 16 * QT), ml_dtypes.bfloat16)
        for j in range(NJ):
            qpos = idx[QT * j:QT * (j + 1)]
            for mloc in range(16):
                kpos = 2048 * j + 128 * mloc + np.arange(128)
                masks[j, :, mloc * QT:(mloc + 1) * QT] = (
                    kpos[:, None] <= qpos[None, :]).astype(ml_dtypes.bfloat16)
        m = dict(shared)
        m.update(
            xT=np.ascontiguousarray(x.T[:, idx]),
            cosk=np.ascontiguousarray(cos),
            sink=np.ascontiguousarray(sin_rot),
            wrow=wrowv,
            masksd=masks,
        )
        in_maps.append(m)

    nc = build(_PHASES)
    res = run_bass_kernel_spmd(nc, in_maps, core_ids=list(range(NCORE)))
    out = np.zeros((T, D), np.float32)
    for c in range(NCORE):
        out[_own_idx(c), :] = res.results[c]["outT"].T
    return out[None]




# version 11
# speedup vs baseline: 1.1418x; 1.1418x over previous
import math
import os
from contextlib import ExitStack

import numpy as np
import ml_dtypes

import concourse.bass as bass
import concourse.mybir as mybir
from concourse.bass_utils import run_bass_kernel_spmd

F32 = mybir.dt.float32
BF16 = mybir.dt.bfloat16
AF = mybir.ActivationFunctionType
ALU = mybir.AluOpType

NCORE = 8
D = 4096
ND = 32
T = 8192
TC = 1024
NH = 32
NKV = 8
HD = 128
NJ = 4
QT = 256
EPS = 1e-5
NQMASK = 7580.0

_PHASES = int(os.environ.get("MK_PHASES", "5"))


def build(ph_max=5):
    nc = bass.Bass("TRN2", target_bir_lowering=False)

    ap = {}
    def din(name, shape, dt=F32):
        ap[name] = nc.dram_tensor(name, shape, dt, kind="ExternalInput").ap()
    def dout(name, shape, dt=F32):
        ap[name] = nc.dram_tensor(name, shape, dt, kind="ExternalOutput").ap()
    def dint(name, shape, dt=F32):
        ap[name] = nc.dram_tensor(name, shape, dt).ap()

    din("xT", [D, TC])
    din("wfT", [D, D]); din("wqT", [D, D]); din("woT", [D, D])
    din("wkT", [D, NKV * HD]); din("wvT", [D, NKV * HD])
    din("bcols", [128, ND])
    din("aT", [D, 128]); din("bT", [128, D])
    din("rwT", [D, 8]); din("rb8", [8, 1]); din("esel", [8, 128])
    din("cosk", [128, TC]); din("sink", [128, TC])
    din("wrow", [1, TC]); din("ident", [128, 128])
    din("masksd", [NJ, 128, 16 * QT], BF16)
    dout("outT", [D, TC])
    dint("x1d", [D, TC]); dint("x2d", [D, TC])
    dint("qd", [NH * HD, TC], BF16)
    dint("kvin", [2 * NKV * TC, HD], BF16)
    dint("kvout", [NCORE * 2 * NKV * TC, HD], BF16)
    dint("ar1i", [128, ND]); dint("ar1o", [128, ND])
    dint("ar2i", [128, ND]); dint("ar2o", [128, ND])
    dint("arki", [1, 8]); dint("arko", [1, 8])
    dint("rowsd", [4, TC])

    st = ExitStack()
    S = {}
    def sbt(name, shape, dt):
        S[name] = st.enter_context(nc.sbuf_tensor(name, shape, dt))
    PS = {}
    def pst(name, shape, dt):
        PS[name] = st.enter_context(nc.psum_tensor(name, shape, dt))
    SEMH = {}
    CNT = {}
    def mksem(name):
        SEMH[name] = st.enter_context(nc.semaphore(name))
        CNT[name] = 0

    with st:
        sbt("xln", [128, ND * TC], BF16)
        sbt("slab0", [128, D], BF16); sbt("slab1", [128, D], BF16)
        sbt("abf", [128, ND * 128], BF16)
        sbt("btf", [128, D], BF16)
        sbt("rwbf", [128, ND * 8], BF16)
        sbt("eselbf", [8, 128], BF16)
        sbt("maskb", [128, 16 * QT], BF16)
        sbt("s_cosk", [128, TC], F32); sbt("s_sink", [128, TC], F32)
        sbt("mb", [128, TC], F32); sbt("rb_", [128, TC], F32)
        sbt("wb", [128, TC], F32); sbt("wrb", [128, TC], F32)
        sbt("xs0", [128, TC], F32); sbt("xs1", [128, TC], F32)
        sbt("xbf0", [128, TC], BF16); sbt("xbf1", [128, TC], BF16)
        sbt("tmp0", [128, TC], F32); sbt("tmp1", [128, TC], F32)
        sbt("xnw0", [128, TC], F32); sbt("xnw1", [128, TC], F32)
        sbt("tbf", [128, TC], BF16)
        sbt("natb0", [128, TC], BF16); sbt("natb1", [128, TC], BF16)
        sbt("qt0", [128, 4 * QT], BF16); sbt("qt1", [128, 4 * QT], BF16)
        for q3 in range(4):
            sbt("ktb%d" % q3, [128, 512], BF16)
            sbt("ktT%d" % q3, [128, 512], BF16)
            sbt("vtb%d" % q3, [128, 512], BF16)
        sbt("qt2", [128, 4 * QT], BF16)
        sbt("pbf0", [128, 4 * QT], BF16); sbt("pbf1", [128, 4 * QT], BF16)
        sbt("recb", [128, TC], F32)
        sbt("onesb", [128, 1], BF16)
        sbt("identb", [128, 128], BF16)
        sbt("bcsb", [128, ND], F32); sbt("b2c", [128, ND], F32)
        sbt("tcol", [128, 1], F32)
        sbt("Cb", [128, NJ * NKV], F32)
        sbt("rbsb", [8, 1], F32)
        sbt("lg", [8, 1], F32); sbt("elg", [8, 1], F32); sbt("elgbf", [8, 1], BF16)
        sbt("rec8", [8, 1], F32); sbt("rt2", [8, 1], F32); sbt("rt2bf", [8, 1], BF16)
        sbt("qsum", [128, ND], F32); sbt("qsumbf", [128, ND], BF16)
        sbt("m_row", [1, TC], F32); sbt("rstd_row", [1, TC], F32)
        sbt("ex2_row", [1, TC], F32)
        sbt("scr_row", [1, TC], F32)
        sbt("kmxr", [1, 16], F32); sbt("kmxg", [1, 16], F32)
        sbt("qmaxr", [1, 128], F32)
        sbt("c2a", [1, 32], F32); sbt("c2b", [1, 32], F32)
        sbt("cpos", [1, 32], F32); sbt("cneg", [1, 32], F32)

        pst("psA", [128, 1024], F32)
        pst("psB", [128, 1024], F32)
        pst("psC", [128, 1024], F32)
        pst("psD", [128, 512], F32)
        pst("psTR", [128, 1024], BF16)

        for n in ("cst", "sG", "sMM", "sACT", "sDVE", "sXD",
                  "sXa", "sXb", "sWa", "sWb",
                  "sKT0", "sKT1", "sKT2", "sKT3",
                  "sVT0", "sVT1", "sVT2", "sVT3",
                  "sQT0", "sQT1", "sQT2", "sMK0", "sMK1",
                  "sSMM", "sEXP", "sMSK", "sPV", "sFIN", "sTRK", "sKTT",
                  "sOEV", "sOTT",
                  "sNB0", "sNB1", "sSTn0", "sSTn1", "sSTx0", "sSTx1",
                  "cAR1", "cAR2", "cARK", "cAG"):
            mksem(n)

        EV = {k: [] for k in ("tensor", "scalar", "vector", "sync", "gpsimd")}
        FREE = {}

        def emit(q, fn):
            EV[q].append(fn)

        def wait(q, s, v):
            if v <= 0:
                return
            emit(q, lambda e, s=s, v=v: e.wait_ge(SEMH[s], v))

        def wfree(q, name):
            if name in FREE:
                s, v = FREE[name]
                wait(q, s, v)

        def op(q, fn, inc=None, n=1):
            if inc is not None:
                CNT[inc] += n
            def f(e, fn=fn, inc=inc, n=n):
                ins = fn(e)
                if inc is not None:
                    ins.then_inc(SEMH[inc], n)
            emit(q, f)
            return CNT[inc] if inc else None

        def dma(q, outap, inap, s, transpose=False, slow=False):
            CNT[s] += 16
            def f(e, o=outap, i=inap, s=s, tr=transpose, sl=slow):
                if tr:
                    e.dma_start_transpose(out=o, in_=i).then_inc(SEMH[s], 16)
                else:
                    e.dma_start(out=o, in_=i,
                                allow_slow_non_contiguous=sl).then_inc(SEMH[s], 16)
            emit(q, f)
            return CNT[s]

        X = S

        def xs(i): return X["xs%d" % (i % 2)]
        def xbf(i): return X["xbf%d" % (i % 2)]
        def xsq(i): return X["pbf%d" % (i % 2)]
        def tmp(i): return X["tmp%d" % (i % 2)]
        def xnw(i): return X["xnw%d" % (i % 2)]
        def slab(i): return X["slab%d" % (i % 2)]
        def evb(i): return X["xbf%d" % (i % 2)]
        def natb(i): return X["natb%d" % (i % 2)]
        def xsem(i): return "sX%s" % ("ab"[i % 2])
        def wsem(i): return "sW%s" % ("ab"[i % 2])
        def stx(i): return "sSTx%d" % (i % 2)

        dma("gpsimd", X["abf"][:].rearrange("p (c r) -> p c r", r=128),
            ap["aT"].rearrange("(c p) r -> p c r", p=128), "cst")
        dma("gpsimd", X["btf"][:], ap["bT"], "cst")
        dma("gpsimd", X["rwbf"][:].rearrange("p (c e) -> p c e", e=8),
            ap["rwT"].rearrange("(c p) e -> p c e", p=128), "cst")
        dma("gpsimd", X["eselbf"][:], ap["esel"], "cst")
        dma("gpsimd", X["identb"][:], ap["ident"], "cst")
        for nm in ("cosk", "sink"):
            dma("gpsimd", X["s_" + nm][:], ap[nm], "cst")
        dma("gpsimd", X["bcsb"][:], ap["bcols"], "cst")
        dma("gpsimd", X["rbsb"][:], ap["rb8"], "cst")
        dma("gpsimd", X["wb"][:], ap["wrow"][0:1, :].broadcast_to([128, TC]), "cst")
        cst_total = CNT["cst"]

        wait("vector", "cst", cst_total)
        op("vector", lambda e: e.memset(X["onesb"][:], 1.0))
        op("vector", lambda e: e.tensor_scalar_mul(X["b2c"][:], X["bcsb"][:], 0.5), "sDVE")

        def stats_mms(o, first, last):
            def f(e, o=o, first=first, last=last):
                e.matmul(PS["psC"][0:1, 0:512], X["onesb"][:], xbf(o)[:, 0:512],
                         start=first, stop=last, skip_group_check=True)
                e.matmul(PS["psC"][0:1, 512:1024], X["onesb"][:], xbf(o)[:, 512:1024],
                         start=first, stop=last, skip_group_check=True)
                e.matmul(PS["psD"][0:1, 0:512], X["onesb"][:], xsq(o)[:, 0:512],
                         start=first, stop=last, skip_group_check=True)
                return e.matmul(PS["psD"][32:33, 0:512], X["onesb"][:],
                                xsq(o)[:, 512:1024],
                                start=first, stop=last, skip_group_check=True)
            v = op("tensor", f, "sMM")
            FREE["xbq%d" % (o % 2)] = ("sMM", v)
            return v

        def ln_finalize(with_wrb):
            ps = PS["psC"]
            wait("vector", "sMM", CNT["sMM"])
            op("vector", lambda e: e.tensor_scalar_mul(X["m_row"][:], ps[0:1, :], 1.0 / D))
            op("vector", lambda e: e.tensor_scalar_mul(
                X["ex2_row"][0:1, 0:512], PS["psD"][0:1, 0:512], 1.0 / D))
            op("vector", lambda e: e.tensor_scalar_mul(
                X["ex2_row"][0:1, 512:1024], PS["psD"][32:33, 0:512], 1.0 / D))
            op("vector", lambda e: e.tensor_mul(X["scr_row"][:], X["m_row"][:], X["m_row"][:]))
            op("vector", lambda e: e.tensor_sub(X["ex2_row"][:], X["ex2_row"][:],
                                                X["scr_row"][:]))
            op("vector", lambda e: e.tensor_scalar_add(X["ex2_row"][:], X["ex2_row"][:],
                                                       EPS), "sDVE")
            FREE["psC"] = ("sDVE", CNT["sDVE"])
            wait("scalar", "sDVE", CNT["sDVE"])
            op("scalar", lambda e: e.activation(X["ex2_row"][:], X["ex2_row"][:], AF.Ln))
            op("scalar", lambda e: e.activation(X["rstd_row"][:], X["ex2_row"][:], AF.Exp,
                                                scale=-0.5), "sACT")
            wait("gpsimd", "sACT", CNT["sACT"])
            dma("gpsimd", ap["rowsd"][0:1, :], X["m_row"][:], "sG")
            dma("gpsimd", ap["rowsd"][1:2, :], X["rstd_row"][:], "sG")
            wait("gpsimd", "sG", CNT["sG"])
            dma("gpsimd", X["mb"][:], ap["rowsd"][0:1, :].broadcast_to([128, TC]), "sG")
            g_done = dma("gpsimd", X["rb_"][:],
                         ap["rowsd"][1:2, :].broadcast_to([128, TC]), "sG")
            wait("vector", "sG", g_done)
            if with_wrb:
                op("vector", lambda e: e.tensor_mul(X["wrb"][:], X["rb_"][:], X["wb"][:]),
                   "sDVE")

        def ln_apply(src_ap, with_qsum, store_counts=None):
            if store_counts is None:
                wait("sync", "sSTx0", CNT["sSTx0"])
                wait("sync", "sSTx1", CNT["sSTx1"])
            for s_ in range(ND):
                if store_counts is not None:
                    wait("sync", stx(s_), store_counts[s_])
                wfree(q="sync", name="xs%d" % (s_ % 2))
                lv = dma("sync", xs(s_)[:], src_ap[128 * s_:128 * (s_ + 1), :], xsem(s_))
                wait("vector", xsem(s_), lv)
                if with_qsum:
                    wfree("vector", "xnw%d" % (s_ % 2))
                op("vector", lambda e, s_=s_: e.tensor_sub(tmp(s_)[:], xs(s_)[:], X["mb"][:]))
                op("vector", lambda e, s_=s_: e.tensor_mul(
                    X["xln"][:, s_ * TC:(s_ + 1) * TC], tmp(s_)[:], X["rb_"][:]))
                if with_qsum:
                    op("vector", lambda e, s_=s_: e.tensor_mul(xnw(s_)[:], tmp(s_)[:],
                                                               X["wrb"][:]))
                    v = op("vector", lambda e, s_=s_: e.tensor_reduce(
                        X["qsum"][:, s_:s_ + 1], xnw(s_)[:], mybir.AxisListType.X,
                        ALU.add), "sDVE")
                else:
                    v = op("vector", lambda e: e.engine_nop(), "sDVE")
                FREE["xs%d" % (s_ % 2)] = ("sDVE", v)
            return CNT["sDVE"]

        def ffn_phase(x_src_ap, x_dst_ap, w_src, car, do_stats):
            store_counts = []
            wait("tensor", "sDVE", CNT["sDVE"])
            def tmm(e):
                ins = None
                for th in range(2):
                    for d_ in range(ND):
                        ins = e.matmul(PS["psC"][:, th * 512:(th + 1) * 512],
                                       X["abf"][:, d_ * 128:(d_ + 1) * 128],
                                       X["xln"][:, d_ * TC + th * 512: d_ * TC + th * 512 + 512],
                                       start=(d_ == 0), stop=(d_ == ND - 1),
                                       skip_group_check=True)
                return ins
            wfree("tensor", "psC")
            op("tensor", tmm, "sMM")
            mm_t = CNT["sMM"]

            aro = ap["ar1o"] if car == "cAR1" else ap["ar2o"]
            def router_chain():
                wait("gpsimd", car, 1)
                gq = dma("gpsimd", X["qsum"][:], aro[:, :], "sG")
                wait("vector", "sG", gq)
                op("vector", lambda e: e.tensor_scalar_mul(X["qsumbf"][:], X["qsum"][:],
                                                           1.0 / NQMASK), "sDVE")
                wait("tensor", "sDVE", CNT["sDVE"])
                def lmm(e):
                    ins = None
                    for d_ in range(ND):
                        ins = e.matmul(PS["psD"][0:8, 0:1],
                                       X["rwbf"][:, d_ * 8:(d_ + 1) * 8],
                                       X["qsumbf"][:, d_:d_ + 1],
                                       start=(d_ == 0), stop=(d_ == ND - 1),
                                       skip_group_check=True)
                    return ins
                op("tensor", lmm, "sMM")
                wait("vector", "sMM", CNT["sMM"])
                op("vector", lambda e: e.tensor_add(X["lg"][:], PS["psD"][0:8, 0:1],
                                                    X["rbsb"][:]), "sDVE")
                wait("scalar", "sDVE", CNT["sDVE"])
                op("scalar", lambda e: e.activation(X["elg"][:], X["lg"][:], AF.Exp), "sACT")
                wait("vector", "sACT", CNT["sACT"])
                op("vector", lambda e: e.tensor_copy(X["elgbf"][:], X["elg"][:]), "sDVE")
                wait("tensor", "sDVE", CNT["sDVE"])
                op("tensor", lambda e: e.matmul(PS["psD"][0:1, 4:5], X["onesb"][0:8, 0:1],
                                                X["elgbf"][:], start=True, stop=True,
                                                skip_group_check=True), "sMM")
                wait("vector", "sMM", CNT["sMM"])
                op("vector", lambda e: e.reciprocal(X["rec8"][0:1, :], PS["psD"][0:1, 4:5]),
                   "sDVE")
                wait("gpsimd", "sDVE", CNT["sDVE"])
                dma("gpsimd", ap["rowsd"][2:3, 0:1], X["rec8"][0:1, :], "sG")
                wait("gpsimd", "sG", CNT["sG"])
                g8 = dma("gpsimd", X["rec8"][:], ap["rowsd"][2:3, 0:1].broadcast_to([8, 1]),
                         "sG")
                wait("vector", "sG", g8)
                op("vector", lambda e: e.tensor_scalar(X["rt2bf"][:], X["elg"][:],
                                                        X["rec8"][:, 0:1], 2.0,
                                                        ALU.mult, ALU.mult), "sDVE")
                wait("tensor", "sDVE", CNT["sDVE"])
                op("tensor", lambda e: e.matmul(PS["psD"][:, 8:9], X["eselbf"][:],
                                                X["rt2bf"][:], start=True, stop=True,
                                                skip_group_check=True), "sMM")
                wait("vector", "sMM", CNT["sMM"])
                op("vector", lambda e: e.tensor_copy(X["tcol"][:], PS["psD"][:, 8:9]))
                wait("vector", "sMM", mm_t)
                v = op("vector", lambda e: e.tensor_single_scalar(
                    X["tbf"][:], PS["psC"][:, :], X["tcol"][:, 0:1], ALU.mult), "sDVE")
                FREE["psC"] = ("sDVE", v)
                return v

            act_base = CNT["sACT"]
            for o in range(ND):
                wfree("gpsimd", "slab%d" % (o % 2))
                lw = dma("gpsimd", slab(o)[:].rearrange("p (c o) -> p c o", o=128),
                         ap[w_src].rearrange("(c p) o -> p c o", p=128)
                         [:, :, 128 * o:128 * (o + 1)], wsem(o))
                wfree("sync", "xs%d" % (o % 2))
                lx = dma("sync", xs(o)[:], x_src_ap[128 * o:128 * (o + 1), :], xsem(o))
                wait("tensor", wsem(o), lw)
                if o >= 2:
                    wait("tensor", "sACT", act_base + o)
                psn = "psA" if o % 2 == 0 else "psB"
                def base_only(e, o=o, psn=psn):
                    ins = None
                    for th in range(2):
                        for d_ in range(ND):
                            ins = e.matmul(PS[psn][:, th * 512:(th + 1) * 512],
                                     slab(o)[:, d_ * 128:(d_ + 1) * 128],
                                     X["xln"][:, d_ * TC + th * 512: d_ * TC + th * 512 + 512],
                                     start=(d_ == 0), stop=False, skip_group_check=True)
                    return ins
                def lora_only(e, o=o, psn=psn):
                    ins = None
                    for th in range(2):
                        ins = e.matmul(PS[psn][:, th * 512:(th + 1) * 512],
                                       X["btf"][:, 128 * o:128 * (o + 1)],
                                       X["tbf"][:, th * 512:(th + 1) * 512],
                                       start=False, stop=True, skip_group_check=True)
                    return ins
                def bmm(e, bo=base_only, lo=lora_only):
                    bo(e)
                    return lo(e)
                if o == 0:
                    op("tensor", base_only)
                    dve_t = router_chain()
                    wait("tensor", "sDVE", dve_t)
                    v = op("tensor", lora_only, "sMM")
                else:
                    v = op("tensor", bmm, "sMM")
                FREE["slab%d" % (o % 2)] = ("sMM", v)
                mm_o = v
                wait("scalar", "sMM", mm_o)
                wfree("scalar", "tmp%d" % (o % 2))
                va = op("scalar", lambda e, o=o, psn=psn: e.activation(
                    tmp(o)[:], PS[psn][:, :], AF.Identity,
                    bias=X["b2c"][:, o:o + 1], scale=0.5), "sACT")
                wait("vector", "sACT", va)
                wait("vector", xsem(o), lx)
                wfree("vector", "xnw%d" % (o % 2))
                vx = op("vector", lambda e, o=o: e.tensor_add(xnw(o)[:], xs(o)[:],
                                                              tmp(o)[:]), "sXD")
                FREE["xs%d" % (o % 2)] = ("sXD", vx)
                FREE["tmp%d" % (o % 2)] = ("sXD", vx)
                wait("sync", "sXD", vx)
                vst = dma("sync", x_dst_ap[128 * o:128 * (o + 1), :], xnw(o)[:], stx(o))
                FREE["xnw%d" % (o % 2)] = (stx(o), vst)
                store_counts.append(vst)
                if do_stats:
                    wfree("vector", "xbq%d" % (o % 2))
                    op("vector", lambda e, o=o: e.tensor_copy(xbf(o)[:], xnw(o)[:]))
                    op("vector", lambda e, o=o: e.tensor_mul(xsq(o)[:], xbf(o)[:],
                                                             xbf(o)[:]), "sDVE")
                    wait("tensor", "sDVE", CNT["sDVE"])
                    stats_mms(o, o == 0, o == ND - 1)
            return store_counts

        wait("tensor", "cst", cst_total)
        for s_ in range(ND):
            wfree("sync", "xs%d" % (s_ % 2))
            lv = dma("sync", xs(s_)[:], ap["xT"][128 * s_:128 * (s_ + 1), :], xsem(s_))
            wait("vector", xsem(s_), lv)
            wfree("vector", "xbq%d" % (s_ % 2))
            op("vector", lambda e, s_=s_: e.tensor_copy(xbf(s_)[:], xs(s_)[:]))
            v = op("vector", lambda e, s_=s_: e.tensor_mul(xsq(s_)[:], xbf(s_)[:],
                                                           xbf(s_)[:]), "sDVE")
            FREE["xs%d" % (s_ % 2)] = ("sDVE", v)
            wait("tensor", "sDVE", v)
            stats_mms(s_, s_ == 0, s_ == ND - 1)
        ln_finalize(True)
        ln_apply(ap["xT"], True)
        wait("gpsimd", "sDVE", CNT["sDVE"])
        dma("gpsimd", ap["ar1i"][:, :], X["qsum"][:], "sG")
        wait("gpsimd", "sG", CNT["sG"])
        op("gpsimd", lambda e: e.collective_compute(
            "AllReduce", ALU.add, replica_groups=[list(range(NCORE))],
            ins=[ap["ar1i"].opt()], outs=[ap["ar1o"].opt()]), "cAR1")

        st1 = None
        if ph_max >= 1:
            st1 = ffn_phase(ap["xT"], ap["x1d"], "wfT", "cAR1", True)

        if ph_max >= 2:
            ln_finalize(False)
            ln_apply(ap["x1d"], False, st1)
            dve_ln2 = CNT["sDVE"]

            for sec in range(2):
                wsrc = "wkT" if sec == 0 else "wvT"
                for g in range(NKV):
                    i = sec * NKV + g
                    wfree("gpsimd", "slab%d" % (i % 2))
                    lw = dma("gpsimd", slab(i)[:].rearrange("p (c o) -> p c o", o=128),
                             ap[wsrc].rearrange("(c p) o -> p c o", p=128)
                             [:, :, 128 * g:128 * (g + 1)], wsem(i))
                    wait("tensor", wsem(i), lw)
                    if i == 0:
                        wait("tensor", "sDVE", dve_ln2)
                    psn = "psA" if i % 2 == 0 else "psB"
                    wfree("tensor", psn)
                    def pmm(e, i=i, psn=psn):
                        ins = None
                        for th in range(2):
                            for d_ in range(ND):
                                ins = e.matmul(PS[psn][:, th * 512:(th + 1) * 512],
                                               slab(i)[:, d_ * 128:(d_ + 1) * 128],
                                               X["xln"][:, d_ * TC + th * 512: d_ * TC + th * 512 + 512],
                                               start=(d_ == 0), stop=(d_ == ND - 1),
                                               skip_group_check=True)
                        return ins
                    v = op("tensor", pmm, "sMM")
                    FREE["slab%d" % (i % 2)] = ("sMM", v)
                    mm_p = v
                    if sec == 0:
                        wait("vector", "sMM", mm_p)
                        wfree("vector", "evb%d" % (g % 2))
                        wfree("vector", "xbq%d" % (g % 2))
                        op("vector", lambda e, psn=psn: e.tensor_mul(
                            tmp(0)[:], PS[psn][:, :], X["s_cosk"][:]))
                        op("vector", lambda e, psn=psn: e.tensor_mul(
                            tmp(1)[0:64, :], PS[psn][64:128, :], X["s_sink"][0:64, :]))
                        op("vector", lambda e, psn=psn: e.tensor_mul(
                            tmp(1)[64:128, :], PS[psn][0:64, :], X["s_sink"][64:128, :]))
                        op("vector", lambda e, g=g: e.tensor_add(
                            evb(g)[:], tmp(0)[:], tmp(1)[:]))
                        wfree("vector", "xbq%d" % (g % 2))
                        v2 = op("vector", lambda e, g=g: e.tensor_mul(
                            xsq(g)[:], evb(g)[:], evb(g)[:]), "sDVE")
                        FREE[psn] = ("sDVE", v2)
                        wait("tensor", "sDVE", v2)
                        def knmm(e, g=g):
                            e.matmul(PS["psC"][0:1, 0:512], X["onesb"][:],
                                     xsq(g)[:, 0:512], start=True, stop=True,
                                     skip_group_check=True)
                            return e.matmul(PS["psC"][0:1, 512:1024], X["onesb"][:],
                                            xsq(g)[:, 512:1024], start=True, stop=True,
                                            skip_group_check=True)
                        vkn = op("tensor", knmm, "sMM")
                        FREE["xbq%d" % (g % 2)] = ("sMM", vkn)
                        wait("vector", "sMM", vkn)
                        op("vector", lambda e, g=g: e.tensor_reduce(
                            X["kmxr"][0:1, g:g + 1], PS["psC"][0:1, :],
                            mybir.AxisListType.X, ALU.max), "sDVE")
                    else:
                        wait("scalar", "sMM", mm_p)
                        wfree("scalar", "evb%d" % (g % 2))
                        va = op("scalar", lambda e, g=g, psn=psn: e.activation(
                            evb(g)[:], PS[psn][:, :], AF.Identity), "sACT")
                        FREE[psn] = ("sACT", va)
                        wait("tensor", "sACT", va)
                    def trs(e, g=g):
                        ins = None
                        for u in range(8):
                            ins = e.transpose(PS["psTR"][:, u * 128:(u + 1) * 128],
                                              evb(g)[:, u * 128:(u + 1) * 128],
                                              X["identb"][:])
                        return ins
                    vtr = op("tensor", trs, "sMM")
                    FREE["evb%d" % (g % 2)] = ("sMM", vtr)
                    wait("vector", "sMM", vtr)
                    wfree("vector", "natb%d" % (g % 2))
                    vnb = op("vector", lambda e, g=g: e.tensor_copy(natb(g)[:],
                                                                    PS["psTR"][:, :]),
                             "sDVE")
                    wait("sync", "sDVE", vnb)
                    vst = dma("sync",
                              ap["kvin"][(sec * NKV + g) * TC:(sec * NKV + g + 1) * TC, :]
                              .rearrange("(s p) h -> p s h", p=128),
                              natb(g)[:].rearrange("p (s h) -> p s h", h=128),
                              "sNB%d" % (g % 2))
                    FREE["natb%d" % (g % 2)] = ("sNB%d" % (g % 2), vst)
            wq_pre = {}
            for h0 in range(2):
                i0 = 2 * NKV + h0
                wfree("gpsimd", "slab%d" % (i0 % 2))
                wq_pre[h0] = dma("gpsimd",
                                 slab(i0)[:].rearrange("p (c o) -> p c o", o=128),
                                 ap["wqT"].rearrange("(c p) o -> p c o", p=128)
                                 [:, :, 128 * h0:128 * (h0 + 1)], wsem(i0))
            wait("gpsimd", "sNB0", CNT["sNB0"])
            wait("gpsimd", "sNB1", CNT["sNB1"])
            op("gpsimd", lambda e: e.collective_compute(
                "AllGather", ALU.bypass, replica_groups=[list(range(NCORE))],
                ins=[ap["kvin"].opt()], outs=[ap["kvout"].opt()]), "cAG")
            wait("gpsimd", "sDVE", CNT["sDVE"])
            dma("gpsimd", ap["arki"][0:1, :], X["kmxr"][0:1, 0:8], "sG")
            wait("gpsimd", "sG", CNT["sG"])
            op("gpsimd", lambda e: e.collective_compute(
                "AllReduce", ALU.max, replica_groups=[list(range(NCORE))],
                ins=[ap["arki"].opt()], outs=[ap["arko"].opt()]), "cARK")

            for h in range(NH):
                i = 2 * NKV + h
                if h < 2:
                    lw = wq_pre[h]
                else:
                    wfree("gpsimd", "slab%d" % (i % 2))
                    lw = dma("gpsimd", slab(i)[:].rearrange("p (c o) -> p c o", o=128),
                             ap["wqT"].rearrange("(c p) o -> p c o", p=128)
                             [:, :, 128 * h:128 * (h + 1)], wsem(i))
                wait("tensor", wsem(i), lw)
                psn = "psA" if i % 2 == 0 else "psB"
                wfree("tensor", psn)
                def qmm(e, i=i, psn=psn):
                    ins = None
                    for th in range(2):
                        for d_ in range(ND):
                            ins = e.matmul(PS[psn][:, th * 512:(th + 1) * 512],
                                           slab(i)[:, d_ * 128:(d_ + 1) * 128],
                                           X["xln"][:, d_ * TC + th * 512: d_ * TC + th * 512 + 512],
                                           start=(d_ == 0), stop=(d_ == ND - 1),
                                           skip_group_check=True)
                    return ins
                v = op("tensor", qmm, "sMM")
                FREE["slab%d" % (i % 2)] = ("sMM", v)
                wait("vector", "sMM", v)
                wfree("vector", "evb%d" % (h % 2))
                wfree("vector", "xbq%d" % (h % 2))
                op("vector", lambda e, psn=psn: e.tensor_mul(
                    tmp(0)[:], PS[psn][:, :], X["s_cosk"][:]))
                op("vector", lambda e, psn=psn: e.tensor_mul(
                    tmp(1)[0:64, :], PS[psn][64:128, :], X["s_sink"][0:64, :]))
                op("vector", lambda e, psn=psn: e.tensor_mul(
                    tmp(1)[64:128, :], PS[psn][0:64, :], X["s_sink"][64:128, :]))
                op("vector", lambda e, h=h: e.tensor_add(
                    evb(h)[:], tmp(0)[:], tmp(1)[:]))
                wfree("vector", "xbq%d" % (h % 2))
                vq = op("vector", lambda e, h=h: e.tensor_mul(
                    xsq(h)[:], evb(h)[:], evb(h)[:]), "sDVE")
                FREE[psn] = ("sDVE", vq)
                wait("sync", "sDVE", vq)
                vst = dma("sync", ap["qd"][128 * h:128 * (h + 1), :], evb(h)[:],
                          "sSTn%d" % (h % 2))
                FREE["evb%d" % (h % 2)] = ("sSTn%d" % (h % 2), vst)
                wait("tensor", "sDVE", vq)
                def qnmm(e, h=h):
                    e.matmul(PS["psC"][32:33, 0:512], X["onesb"][:],
                             xsq(h)[:, 0:512], start=True, stop=True,
                             skip_group_check=True)
                    return e.matmul(PS["psC"][32:33, 512:1024], X["onesb"][:],
                                    xsq(h)[:, 512:1024], start=True, stop=True,
                                    skip_group_check=True)
                vqn = op("tensor", qnmm, "sMM")
                FREE["xbq%d" % (h % 2)] = ("sMM", vqn)
                wait("vector", "sMM", vqn)
                for j in range(NJ):
                    op("vector", lambda e, h=h, j=j: e.tensor_reduce(
                        X["qmaxr"][0:1, h * 4 + j:h * 4 + j + 1],
                        PS["psC"][32:33, j * QT:(j + 1) * QT],
                        mybir.AxisListType.X, ALU.max),
                        "sDVE" if j == NJ - 1 else None)

        if ph_max >= 3:
            wait("gpsimd", "cARK", 1)
            gk = dma("gpsimd", X["kmxg"][0:1, 0:8], ap["arko"][0:1, :], "sG")
            wait("vector", "sG", gk)
            wait("vector", "sDVE", CNT["sDVE"])
            for g in range(NKV):
                for j in range(NJ):
                    idx = g * NJ + j
                    op("vector", lambda e, g=g, j=j, idx=idx: e.tensor_reduce(
                        X["c2a"][0:1, idx:idx + 1],
                        X["qmaxr"][0:1, :].rearrange("a (h b) -> a b h", b=4)
                        [:, j:j + 1, 4 * g:4 * g + 4],
                        mybir.AxisListType.X, ALU.max))
                    op("vector", lambda e, g=g, idx=idx: e.tensor_single_scalar(
                        X["c2b"][0:1, idx:idx + 1], X["c2a"][0:1, idx:idx + 1],
                        X["kmxg"][0:1, g:g + 1], ALU.mult),
                        "sDVE" if idx == NKV * NJ - 1 else None)
            wait("scalar", "sDVE", CNT["sDVE"])
            op("scalar", lambda e: e.activation(X["cpos"][:], X["c2b"][:], AF.Sqrt),
               "sACT")
            wait("vector", "sACT", CNT["sACT"])
            op("vector", lambda e: e.tensor_scalar_mul(X["cneg"][:], X["cpos"][:], -0.08838834764831845),
               "sDVE")
            wait("gpsimd", "sDVE", CNT["sDVE"])
            dma("gpsimd", ap["rowsd"][3:4, 0:32], X["cneg"][:], "sG")
            wait("gpsimd", "sG", CNT["sG"])
            gcb = dma("gpsimd", X["Cb"][:],
                      ap["rowsd"][3:4, 0:32].broadcast_to([128, 32]), "sG")
            wait("scalar", "sG", gcb)

            wait("sync", "cAG", 1)
            wait("sync", "sSTn0", CNT["sSTn0"])
            wait("sync", "sSTn1", CNT["sSTn1"])
            kvo = ap["kvout"].rearrange("(r blk row) h -> r blk row h",
                                        blk=2 * NKV, row=TC)

            gM = 0
            exp_of = {}
            pready = {}
            pv_of = {}
            smm_of = {}
            kload = {}
            copy_of = {}
            sup_desc = []
            for j_ in range(NJ):
                for g_ in range(NKV):
                    for u_ in range(4 * (j_ + 1)):
                        sup_desc.append((j_, g_, u_))
            NSUP = len(sup_desc)

            def emit_super_load(sg):
                j_, g_, u_ = sup_desc[sg]
                sl = sg % 4
                wfree("sync", "ktb%d" % sl)
                ktv = vtv = None
                for r2_ in range(2):
                    ktv = dma("sync",
                              X["ktb%d" % sl][64 * r2_:64 * (r2_ + 1), :]
                              .rearrange("p (kc h) -> p kc h", h=128),
                              kvo[r2_::2, g_, 64 * u_:64 * u_ + 64, :]
                              .rearrange("kc u_ h -> u_ kc h"),
                              "sKT%d" % sl)
                wfree("sync", "vtb%d" % sl)
                for r2_ in range(2):
                    vtv = dma("sync",
                              X["vtb%d" % sl][64 * r2_:64 * (r2_ + 1), :]
                              .rearrange("p (kc h) -> p kc h", h=128),
                              kvo[r2_::2, NKV + g_, 64 * u_:64 * u_ + 64, :]
                              .rearrange("kc u_ h -> u_ kc h"),
                              "sVT%d" % sl)
                kload[sg] = (sl, ktv, vtv)

            def emit_super_transpose(sg):
                sl, ktv, vtv = kload[sg]
                wait("tensor", "sKT%d" % sl, ktv)
                if sg >= 1:
                    wait("tensor", "sKTT", copy_of[sg - 1])
                def ktr(e, sl=sl):
                    ins = None
                    for kc_ in range(4):
                        ins = e.transpose(
                            PS["psTR"][:, kc_ * 128:(kc_ + 1) * 128],
                            X["ktb%d" % sl][:, kc_ * 128:(kc_ + 1) * 128],
                            X["identb"][:])
                    return ins
                vtr = op("tensor", ktr, "sTRK")
                FREE["ktb%d" % sl] = ("sTRK", vtr)
                wait("scalar", "sTRK", vtr)
                wfree("scalar", "ktT%d" % sl)
                copy_of[sg] = op("scalar", lambda e, sl=sl: e.activation(
                    X["ktT%d" % sl][:], PS["psTR"][:, 0:512], AF.Identity), "sKTT")

            emit_super_load(0)
            emit_super_transpose(0)
            gsup = 0
            for j in range(NJ):
                wfree("sync", "maskb")
                mload = dma("sync", X["maskb"][:], ap["masksd"][j], "sMK0")
                for g in range(NKV):
                    gi = j * NKV + g
                    CL = 16 * (j + 1)
                    wfree("sync", "qt%d" % (gi % 3))
                    lq = dma("sync", X["qt%d" % (gi % 3)][:].rearrange(
                        "p (b c) -> p b c", c=QT),
                        ap["qd"].rearrange("(b p) t -> p b t", p=128)
                        [:, 4 * g:4 * g + 4, QT * j:QT * (j + 1)],
                        "sQT%d" % (gi % 3))
                    first_chunk = gM

                    def emit_pv(M, cid, g=g, CL=CL):
                        sslot = kload[cid // 4][0]
                        psem, pval = pready[cid]
                        wait("tensor", psem, pval)
                        if M == 0:
                            wait("tensor", "sFIN", CNT["sFIN"])
                            wait("tensor", "sOEV", CNT["sOEV"])
                        def pvm(e, cid=cid, kc=cid % 4, M=M, CL=CL, sslot=sslot):
                            for b2 in range(2):
                                e.matmul(PS["psC"][:, b2 * 512:(b2 + 1) * 512],
                                         X["vtb%d" % sslot][:, kc * 128:(kc + 1) * 128],
                                         X["pbf%d" % (cid % 2)][:, b2 * 512:(b2 + 1) * 512],
                                         start=(M == 0), stop=(M == CL - 1),
                                         skip_group_check=True)
                            e.matmul(PS["psD"][0:1, 0:512], X["onesb"][:],
                                     X["pbf%d" % (cid % 2)][:, 0:512],
                                     start=(M == 0), stop=(M == CL - 1),
                                     skip_group_check=True)
                            return e.matmul(PS["psD"][32:33, 0:512], X["onesb"][:],
                                            X["pbf%d" % (cid % 2)][:, 512:1024],
                                            start=(M == 0), stop=(M == CL - 1),
                                            skip_group_check=True)
                        v = op("tensor", pvm, "sPV")
                        pv_of[cid] = v
                        if cid % 4 == 3:
                            FREE["vtb%d" % (kload[cid // 4][0])] = ("sPV", v)

                    for M in range(CL):
                        kc = M % 4
                        if kc == 0:
                            su = gsup
                            if su + 1 < NSUP:
                                emit_super_load(su + 1)
                                emit_super_transpose(su + 1)
                            gsup += 1
                        sps = "psA" if gM % 2 == 0 else "psB"
                        if gM < 2:
                            wfree("tensor", sps)
                        if kc == 0:
                            wait("tensor", "sKTT", copy_of[su])
                            wait("tensor", "sVT%d" % (kload[su][0]), kload[su][2])
                        if M == 0:
                            wait("tensor", "sQT%d" % (gi % 3), lq)
                        if gM >= 2:
                            wait("tensor", "sEXP", exp_of[gM - 2])
                        def smm(e, sps=sps, sslot=kload[su][0], kc=kc, gi=gi):
                            ins = None
                            for b2 in range(2):
                                ins = e.matmul(
                                    PS[sps][:, b2 * 512:(b2 + 1) * 512],
                                    X["ktT%d" % sslot][:, kc * 128:(kc + 1) * 128],
                                    X["qt%d" % (gi % 3)][:, b2 * 512:(b2 + 1) * 512],
                                    start=True, stop=True, skip_group_check=True)
                            return ins
                        vs = op("tensor", smm, "sSMM")
                        smm_of[gM] = vs
                        if kc == 3:
                            FREE["ktT%d" % (kload[su][0])] = ("sSMM", vs)
                        wait("scalar", "sSMM", vs)
                        if gM >= 2:
                            wait("scalar", "sPV", pv_of[gM - 2])
                        ve = op("scalar", lambda e, sps=sps, gM=gM, g=g, j=j:
                                e.activation(X["pbf%d" % (gM % 2)][:], PS[sps][:, :],
                                             AF.Exp, scale=0.08838834764831845,
                                             bias=X["Cb"][:, g * NJ + j:g * NJ + j + 1]),
                                "sEXP")
                        exp_of[gM] = ve
                        if M >= CL - 16:
                            moff = (M - (CL - 16)) * QT
                            wait("vector", "sEXP", ve)
                            if M == CL - 16 and g == 0:
                                wait("vector", "sMK0", mload)
                            def msk(e, gM=gM, moff=moff):
                                ins = None
                                for b in range(4):
                                    ins = e.tensor_mul(
                                        X["pbf%d" % (gM % 2)][:, b * QT:(b + 1) * QT],
                                        X["pbf%d" % (gM % 2)][:, b * QT:(b + 1) * QT],
                                        X["maskb"][:, moff:moff + QT])
                                return ins
                            vm = op("vector", msk, "sMSK")
                            pready[gM] = ("sMSK", vm)
                        else:
                            pready[gM] = ("sEXP", ve)
                        if M >= 1:
                            emit_pv(M - 1, gM - 1)
                        gM += 1
                    emit_pv(CL - 1, gM - 1)
                    FREE["qt%d" % (gi % 3)] = ("sSMM", smm_of[gM - 1])
                    if g == NKV - 1:
                        FREE["maskb"] = ("sMSK", CNT["sMSK"])
                    wait("scalar", "sPV", pv_of[gM - 1])
                    if gi >= 1:
                        wait("scalar", "sOTT", gi)
                    vo = op("scalar", lambda e: e.activation(
                        X["tbf"][:], PS["psC"][:, :], AF.Identity), "sOEV")
                    wait("vector", "sPV", pv_of[gM - 1])
                    op("vector", lambda e: e.reciprocal(X["scr_row"][0:1, 0:512],
                                                        PS["psD"][0:1, 0:512]))
                    op("vector", lambda e: e.reciprocal(X["scr_row"][0:1, 512:1024],
                                                        PS["psD"][32:33, 0:512]),
                       "sFIN")
                    wait("gpsimd", "sFIN", CNT["sFIN"])
                    dma("gpsimd", ap["rowsd"][2:3, :], X["scr_row"][:], "sG")
                    wait("gpsimd", "sG", CNT["sG"])
                    grb = dma("gpsimd", X["recb"][:],
                              ap["rowsd"][2:3, :].broadcast_to([128, TC]), "sG")
                    wait("vector", "sG", grb)
                    wait("vector", "sOEV", vo)
                    op("vector", lambda e, g=g, j=j: e.tensor_mul(
                        X["xln"][:].rearrange("p (hh t) -> p hh t", t=TC)
                        [:, 4 * g:4 * g + 4, QT * j:QT * (j + 1)],
                        X["tbf"][:].rearrange("p (h c) -> p h c", c=QT),
                        X["recb"][:].rearrange("p (h c) -> p h c", c=QT)), "sOTT")

            st2 = []
            fin_all = CNT["sOTT"]
            for o in range(ND):
                wfree("gpsimd", "slab%d" % (o % 2))
                lw = dma("gpsimd", slab(o)[:].rearrange("p (c o) -> p c o", o=128),
                         ap["woT"].rearrange("(c p) o -> p c o", p=128)
                         [:, :, 128 * o:128 * (o + 1)], wsem(o))
                if o == 0:
                    wait("sync", "sSTx0", CNT["sSTx0"])
                    wait("sync", "sSTx1", CNT["sSTx1"])
                wfree("sync", "xs%d" % (o % 2))
                lx = dma("sync", xs(o)[:], ap["x1d"][128 * o:128 * (o + 1), :], xsem(o))
                wait("tensor", wsem(o), lw)
                if o == 0:
                    wait("tensor", "sOTT", fin_all)
                psn = "psA" if o % 2 == 0 else "psB"
                wfree("tensor", psn)
                def wmm(e, o=o, psn=psn):
                    ins = None
                    for th in range(2):
                        for d_ in range(ND):
                            ins = e.matmul(PS[psn][:, th * 512:(th + 1) * 512],
                                           slab(o)[:, d_ * 128:(d_ + 1) * 128],
                                           X["xln"][:, d_ * TC + th * 512: d_ * TC + th * 512 + 512],
                                           start=(d_ == 0), stop=(d_ == ND - 1),
                                           skip_group_check=True)
                    return ins
                v = op("tensor", wmm, "sMM")
                FREE["slab%d" % (o % 2)] = ("sMM", v)
                wait("vector", "sMM", v)
                wait("vector", xsem(o), lx)
                wfree("vector", "xnw%d" % (o % 2))
                vx = op("vector", lambda e, o=o, psn=psn: e.tensor_add(
                    xnw(o)[:], PS[psn][:, :], xs(o)[:]), "sXD")
                FREE[psn] = ("sXD", vx)
                FREE["xs%d" % (o % 2)] = ("sXD", vx)
                wait("sync", "sXD", vx)
                vst = dma("sync", ap["x2d"][128 * o:128 * (o + 1), :], xnw(o)[:], stx(o))
                FREE["xnw%d" % (o % 2)] = (stx(o), vst)
                st2.append(vst)
                wfree("vector", "xbq%d" % (o % 2))
                op("vector", lambda e, o=o: e.tensor_copy(xbf(o)[:], xnw(o)[:]))
                op("vector", lambda e, o=o: e.tensor_mul(xsq(o)[:], xbf(o)[:], xbf(o)[:]),
                   "sDVE")
                wait("tensor", "sDVE", CNT["sDVE"])
                stats_mms(o, o == 0, o == ND - 1)

        if ph_max >= 4:
            ln_finalize(True)
            ln_apply(ap["x2d"], True, st2)
            wait("gpsimd", "sDVE", CNT["sDVE"])
            dma("gpsimd", ap["ar2i"][:, :], X["qsum"][:], "sG")
            wait("gpsimd", "sG", CNT["sG"])
            op("gpsimd", lambda e: e.collective_compute(
                "AllReduce", ALU.add, replica_groups=[list(range(NCORE))],
                ins=[ap["ar2i"].opt()], outs=[ap["ar2o"].opt()]), "cAR2")
            ffn_phase(ap["x2d"], ap["outT"], "wfT", "cAR2", False)

        if os.environ.get("MK_DEBUG2") and ph_max >= 2:
            dout("dbg_qd", [NH * HD, TC], BF16)
            dout("dbg_kvin", [2 * NKV * TC, HD], BF16)
            dout("dbg_kmx", [1, 16])
            dout("dbg_qmax", [1, 128])
            dout("dbg_cb", [128, NJ * NKV])
            wait("gpsimd", "sSTn0", CNT["sSTn0"])
            wait("gpsimd", "sSTn1", CNT["sSTn1"])
            wait("gpsimd", "sNB0", CNT["sNB0"])
            wait("gpsimd", "sNB1", CNT["sNB1"])
            wait("gpsimd", "sDVE", CNT["sDVE"])
            dma("gpsimd", ap["dbg_qd"], ap["qd"], "sG")
            dma("gpsimd", ap["dbg_kvin"], ap["kvin"], "sG")
            dma("gpsimd", ap["dbg_kmx"], X["kmxg"][:] if ph_max >= 3 else X["kmxr"][:], "sG")
            dma("gpsimd", ap["dbg_qmax"], X["qmaxr"][:], "sG")
            if ph_max >= 3:
                dma("gpsimd", ap["dbg_cb"], X["Cb"][:], "sG")

        if os.environ.get("MK_DEBUG3") and ph_max >= 3:
            dout("dbg_obuf", [128, ND * TC], BF16)
            wait("gpsimd", "sOTT", CNT["sOTT"])
            dma("gpsimd", ap["dbg_obuf"], X["xln"][:], "sG")

        if os.environ.get("MK_DEBUG"):
            dbg = {}
            dout("dbg_rows", [4, TC]); dout("dbg_xln", [128, TC], BF16)
            dout("dbg_cols", [128, 8]); dout("dbg_tbf", [128, TC], BF16)
            dout("dbg_small", [8, 4])
            wait("gpsimd", "sDVE", CNT["sDVE"])
            wait("gpsimd", "sMM", CNT["sMM"])
            wait("gpsimd", "sACT", CNT["sACT"])
            dma("gpsimd", ap["dbg_rows"][0:1, :], X["m_row"][:], "sG")
            dma("gpsimd", ap["dbg_rows"][1:2, :], X["rstd_row"][:], "sG")
            dma("gpsimd", ap["dbg_rows"][2:3, :], X["ex2_row"][:], "sG")
            dma("gpsimd", ap["dbg_rows"][3:4, :], X["scr_row"][:], "sG")
            dma("gpsimd", ap["dbg_xln"], X["xln"][:, 0:TC], "sG")
            dma("gpsimd", ap["dbg_cols"][:, 0:1], X["tcol"][:], "sG", slow=True)
            dma("gpsimd", ap["dbg_cols"][:, 1:2], X["b2c"][:, 0:1], "sG", slow=True)
            dma("gpsimd", ap["dbg_cols"][:, 2:3], X["wb"][:, 0:1], "sG", slow=True)
            dma("gpsimd", ap["dbg_cols"][:, 3:4], X["wrb"][:, 0:1], "sG", slow=True)
            dma("gpsimd", ap["dbg_cols"][:, 4:5], X["qsum"][:, 0:1], "sG", slow=True)
            dma("gpsimd", ap["dbg_tbf"], X["tbf"][:], "sG")
            dma("gpsimd", ap["dbg_small"][:, 0:1], X["lg"][:], "sG", slow=True)
            dma("gpsimd", ap["dbg_small"][:, 1:2], X["elg"][:], "sG", slow=True)
            dma("gpsimd", ap["dbg_small"][:, 2:3], X["rt2"][:], "sG", slow=True)
            dma("gpsimd", ap["dbg_small"][:, 3:4], X["rec8"][:], "sG", slow=True)

        if ph_max < 4:
            src = ap["x1d"] if ph_max < 3 else ap["x2d"]
            wait("gpsimd", "sSTx0", CNT["sSTx0"])
            wait("gpsimd", "sSTx1", CNT["sSTx1"])
            wait("gpsimd", "sDVE", CNT["sDVE"])
            wait("gpsimd", "sMM", CNT["sMM"])
            dma("gpsimd", ap["outT"], src, "sG")

        wait("sync", "sSTx0", CNT["sSTx0"])
        wait("sync", "sSTx1", CNT["sSTx1"])
        wait("gpsimd", "sG", CNT["sG"])

        with nc.Block() as block:
            @block.tensor
            def _(t):
                for f in EV["tensor"]:
                    f(t)

            @block.scalar
            def _(s_):
                for f in EV["scalar"]:
                    f(s_)

            @block.vector
            def _(v):
                for f in EV["vector"]:
                    f(v)

            @block.sync
            def _(sy):
                for f in EV["sync"]:
                    f(sy)

            @block.gpsimd
            def _(g):
                for f in EV["gpsimd"]:
                    f(g)

        if os.environ.get("MK_VERBOSE"):
            print("[mk] events:", {k: len(v) for k, v in EV.items()})
    return nc


def _llama3_inv_freq():
    ROPE_THETA, ROPE_FACTOR, HIGH_FF, LOW_FF, OMP = 500000.0, 8.0, 4.0, 1.0, 8192
    inv = 1.0 / ROPE_THETA ** (np.arange(0, HD, 2, dtype=np.float32) / HD)
    wavelen = 2.0 * np.pi / inv
    low_wl = OMP / LOW_FF
    high_wl = OMP / HIGH_FF
    inv_scaled = np.where(wavelen > low_wl, inv / ROPE_FACTOR, inv)
    smooth = (OMP / wavelen - LOW_FF) / (HIGH_FF - LOW_FF)
    smoothed = (1.0 - smooth) / ROPE_FACTOR * inv + smooth * inv
    is_medium = (wavelen <= low_wl) & (wavelen >= high_wl)
    return np.where(is_medium, smoothed, inv_scaled).astype(np.float32)


def _own_idx(c):
    bo = np.arange(16)
    u = np.arange(64)
    return (64 * (c + 8 * bo[:, None]) + u[None, :]).reshape(-1)


def kernel(L0_attn_x, L0_attn_Wq, L0_attn_Wk, L0_attn_Wv, L0_attn_Wo,
           L0_ffn_W, L0_ffn_b, L0_ffn_A, L0_ffn_B,
           L0_ffn_router_W, L0_ffn_router_b):
    f32 = np.float32
    x = np.ascontiguousarray(np.asarray(L0_attn_x, dtype=f32)[0])
    shared = dict(
        wfT=np.ascontiguousarray(np.asarray(L0_ffn_W, f32).T),
        wqT=np.ascontiguousarray(np.asarray(L0_attn_Wq, f32).T),
        wkT=np.ascontiguousarray(np.asarray(L0_attn_Wk, f32).T),
        wvT=np.ascontiguousarray(np.asarray(L0_attn_Wv, f32).T),
        woT=np.ascontiguousarray(np.asarray(L0_attn_Wo, f32).T),
        bcols=np.ascontiguousarray(np.asarray(L0_ffn_b, f32).reshape(ND, 128).T),
        aT=np.ascontiguousarray(np.asarray(L0_ffn_A, f32).reshape(128, D).T),
        bT=np.ascontiguousarray(
            np.asarray(L0_ffn_B, f32).transpose(0, 2, 1).reshape(128, D)),
        rwT=np.ascontiguousarray(np.asarray(L0_ffn_router_W, f32).T),
        rb8=np.ascontiguousarray(np.asarray(L0_ffn_router_b, f32).reshape(8, 1)),
        esel=np.kron(np.eye(8, dtype=f32), np.ones((1, 16), f32)),
        ident=np.eye(128, dtype=f32),
    )
    invf = _llama3_inv_freq()
    scale = f32(1.0 / math.sqrt(HD))
    in_maps = []
    for c in range(NCORE):
        idx = _own_idx(c)
        pos = idx.astype(np.float32)
        fr = pos[None, :] * invf[:, None]
        cos = np.concatenate([np.cos(fr), np.cos(fr)], axis=0).astype(f32)
        sin = np.sin(fr).astype(f32)
        sin_rot = np.concatenate([-sin, sin], axis=0).astype(f32)
        wrowv = ((idx >= 611) & (idx <= 8190)).astype(f32).reshape(1, TC)
        masks = np.zeros((NJ, 128, 16 * QT), ml_dtypes.bfloat16)
        for j in range(NJ):
            qpos = idx[QT * j:QT * (j + 1)]
            for mloc in range(16):
                kpos = 2048 * j + 128 * mloc + np.arange(128)
                masks[j, :, mloc * QT:(mloc + 1) * QT] = (
                    kpos[:, None] <= qpos[None, :]).astype(ml_dtypes.bfloat16)
        m = dict(shared)
        m.update(
            xT=np.ascontiguousarray(x.T[:, idx]),
            cosk=np.ascontiguousarray(cos),
            sink=np.ascontiguousarray(sin_rot),
            wrow=wrowv,
            masksd=masks,
        )
        in_maps.append(m)

    nc = build(_PHASES)
    res = run_bass_kernel_spmd(nc, in_maps, core_ids=list(range(NCORE)))
    out = np.zeros((T, D), np.float32)
    for c in range(NCORE):
        out[_own_idx(c), :] = res.results[c]["outT"].T
    return out[None]




# version 14 (final)
# speedup vs baseline: 1.1602x; 1.0162x over previous
import math
import os
from contextlib import ExitStack

import numpy as np
import ml_dtypes

import concourse.bass as bass
import concourse.mybir as mybir
from concourse.bass_utils import run_bass_kernel_spmd

F32 = mybir.dt.float32
BF16 = mybir.dt.bfloat16
AF = mybir.ActivationFunctionType
ALU = mybir.AluOpType

NCORE = 8
D = 4096
ND = 32
T = 8192
TC = 1024
NH = 32
NKV = 8
HD = 128
NJ = 4
QT = 256
EPS = 1e-5
NQMASK = 7580.0

_PHASES = int(os.environ.get("MK_PHASES", "5"))


def build(ph_max=5):
    nc = bass.Bass("TRN2", target_bir_lowering=False)

    ap = {}
    def din(name, shape, dt=F32):
        ap[name] = nc.dram_tensor(name, shape, dt, kind="ExternalInput").ap()
    def dout(name, shape, dt=F32):
        ap[name] = nc.dram_tensor(name, shape, dt, kind="ExternalOutput").ap()
    def dint(name, shape, dt=F32):
        ap[name] = nc.dram_tensor(name, shape, dt).ap()

    din("xT", [D, TC])
    din("wfT", [D, D]); din("wqT", [D, D]); din("woT", [D, D])
    din("wkT", [D, NKV * HD]); din("wvT", [D, NKV * HD])
    din("bcols", [128, ND])
    din("aT", [D, 128]); din("bT", [128, D])
    din("rwT", [D, 8]); din("rb8", [8, 1]); din("esel", [8, 128])
    din("cosk", [128, TC]); din("sink", [128, TC])
    din("wrow", [1, TC]); din("ident", [128, 128])
    din("masksd", [NJ, 128, 16 * QT], BF16)
    dout("outT", [D, TC])
    dint("x1d", [D, TC]); dint("x2d", [D, TC])
    dint("qd", [NH * HD, TC], BF16)
    dint("kvin", [2 * NKV * TC, HD], BF16)
    dint("kvout", [NCORE * 2 * NKV * TC, HD], BF16)
    dint("ar1i", [128, ND]); dint("ar1o", [128, ND])
    dint("ar2i", [128, ND]); dint("ar2o", [128, ND])
    dint("arki", [1, 8]); dint("arko", [1, 8])
    dint("rowsd", [4, TC])

    st = ExitStack()
    S = {}
    def sbt(name, shape, dt):
        S[name] = st.enter_context(nc.sbuf_tensor(name, shape, dt))
    PS = {}
    def pst(name, shape, dt):
        PS[name] = st.enter_context(nc.psum_tensor(name, shape, dt))
    SEMH = {}
    CNT = {}
    def mksem(name):
        SEMH[name] = st.enter_context(nc.semaphore(name))
        CNT[name] = 0

    with st:
        sbt("xln", [128, ND * TC], BF16)
        sbt("slab0", [128, D], BF16); sbt("slab1", [128, D], BF16)
        sbt("abf", [128, ND * 128], BF16)
        sbt("btf", [128, D], BF16)
        sbt("rwbf", [128, ND * 8], BF16)
        sbt("eselbf", [8, 128], BF16)
        sbt("maskb", [128, 16 * QT], BF16)
        sbt("s_cosk", [128, TC], F32); sbt("s_sink", [128, TC], F32)
        sbt("mb", [128, TC], F32); sbt("rb_", [128, TC], F32)
        sbt("wb", [128, TC], F32); sbt("wrb", [128, TC], F32)
        sbt("xs0", [128, TC], F32); sbt("xs1", [128, TC], F32)
        sbt("xbf0", [128, TC], BF16); sbt("xbf1", [128, TC], BF16)
        sbt("tmp0", [128, TC], F32); sbt("tmp1", [128, TC], F32)
        sbt("xnw0", [128, TC], F32); sbt("xnw1", [128, TC], F32)
        sbt("tbf", [128, TC], BF16)
        sbt("natb0", [128, TC], BF16); sbt("natb1", [128, TC], BF16)
        sbt("qt0", [128, 4 * QT], BF16); sbt("qt1", [128, 4 * QT], BF16)
        for q3 in range(4):
            sbt("ktb%d" % q3, [128, 512], BF16)
            sbt("ktT%d" % q3, [128, 512], BF16)
            sbt("vtb%d" % q3, [128, 512], BF16)
        sbt("qt2", [128, 4 * QT], BF16)
        sbt("pbf0", [128, 4 * QT], BF16); sbt("pbf1", [128, 4 * QT], BF16)
        sbt("recb", [128, TC], F32)
        sbt("onesb", [128, 1], BF16)
        sbt("identb", [128, 128], BF16)
        sbt("bcsb", [128, ND], F32); sbt("b2c", [128, ND], F32)
        sbt("tcol", [128, 1], F32)
        sbt("Cb", [128, NJ * NKV], F32)
        sbt("rbsb", [8, 1], F32)
        sbt("lg", [8, 1], F32); sbt("elg", [8, 1], F32); sbt("elgbf", [8, 1], BF16)
        sbt("rec8", [8, 1], F32); sbt("rt2", [8, 1], F32); sbt("rt2bf", [8, 1], BF16)
        sbt("qsum", [128, ND], F32); sbt("qsumbf", [128, ND], BF16)
        sbt("m_row", [1, TC], F32); sbt("rstd_row", [1, TC], F32)
        sbt("ex2_row", [1, TC], F32)
        sbt("scr_row", [1, TC], F32)
        sbt("kmxr", [1, 16], F32); sbt("kmxg", [1, 16], F32)
        sbt("qmaxr", [1, 128], F32)
        sbt("c2a", [1, 32], F32); sbt("c2b", [1, 32], F32)
        sbt("cpos", [1, 32], F32); sbt("cneg", [1, 32], F32)

        pst("psA", [128, 1024], F32)
        pst("psB", [128, 1024], F32)
        pst("psC", [128, 1024], F32)
        pst("psD", [128, 512], F32)
        pst("psTR", [128, 1024], BF16)

        for n in ("cst", "sG", "sMM", "sACT", "sDVE", "sXD",
                  "sXa", "sXb", "sWa", "sWb",
                  "sKT0", "sKT1", "sKT2", "sKT3",
                  "sVT0", "sVT1", "sVT2", "sVT3",
                  "sQT0", "sQT1", "sQT2", "sMK0", "sMK1",
                  "sSMM", "sEXP", "sMSK", "sPV", "sFIN", "sTRK", "sKTT",
                  "sOEV", "sOTT",
                  "sNB0", "sNB1", "sSTn0", "sSTn1", "sSTx0", "sSTx1",
                  "cAR1", "cAR2", "cARK", "cAG"):
            mksem(n)

        EV = {k: [] for k in ("tensor", "scalar", "vector", "sync", "gpsimd")}
        FREE = {}

        def emit(q, fn):
            EV[q].append(fn)

        def wait(q, s, v):
            if v <= 0:
                return
            emit(q, lambda e, s=s, v=v: e.wait_ge(SEMH[s], v))

        def wfree(q, name):
            if name in FREE:
                s, v = FREE[name]
                wait(q, s, v)

        def op(q, fn, inc=None, n=1):
            if inc is not None:
                CNT[inc] += n
            def f(e, fn=fn, inc=inc, n=n):
                ins = fn(e)
                if inc is not None:
                    ins.then_inc(SEMH[inc], n)
            emit(q, f)
            return CNT[inc] if inc else None

        def dma(q, outap, inap, s, transpose=False, slow=False):
            CNT[s] += 16
            def f(e, o=outap, i=inap, s=s, tr=transpose, sl=slow):
                if tr:
                    e.dma_start_transpose(out=o, in_=i).then_inc(SEMH[s], 16)
                else:
                    e.dma_start(out=o, in_=i,
                                allow_slow_non_contiguous=sl).then_inc(SEMH[s], 16)
            emit(q, f)
            return CNT[s]

        X = S

        def xs(i): return X["xs%d" % (i % 2)]
        def xbf(i): return X["xbf%d" % (i % 2)]
        def xsq(i): return X["pbf%d" % (i % 2)]
        def tmp(i): return X["tmp%d" % (i % 2)]
        def xnw(i): return X["xnw%d" % (i % 2)]
        def slab(i): return X["slab%d" % (i % 2)]
        def evb(i): return X["xbf%d" % (i % 2)]
        def natb(i): return X["natb%d" % (i % 2)]
        def xsem(i): return "sX%s" % ("ab"[i % 2])
        def wsem(i): return "sW%s" % ("ab"[i % 2])
        def stx(i): return "sSTx%d" % (i % 2)

        dma("gpsimd", X["abf"][:].rearrange("p (c r) -> p c r", r=128),
            ap["aT"].rearrange("(c p) r -> p c r", p=128), "cst")
        dma("gpsimd", X["btf"][:], ap["bT"], "cst")
        dma("gpsimd", X["rwbf"][:].rearrange("p (c e) -> p c e", e=8),
            ap["rwT"].rearrange("(c p) e -> p c e", p=128), "cst")
        dma("gpsimd", X["eselbf"][:], ap["esel"], "cst")
        dma("gpsimd", X["identb"][:], ap["ident"], "cst")
        for nm in ("cosk", "sink"):
            dma("gpsimd", X["s_" + nm][:], ap[nm], "cst")
        dma("gpsimd", X["bcsb"][:], ap["bcols"], "cst")
        dma("gpsimd", X["rbsb"][:], ap["rb8"], "cst")
        dma("gpsimd", X["wb"][:], ap["wrow"][0:1, :].broadcast_to([128, TC]), "cst")
        cst_total = CNT["cst"]

        wait("vector", "cst", cst_total)
        op("vector", lambda e: e.memset(X["onesb"][:], 1.0))
        op("vector", lambda e: e.tensor_scalar_mul(X["b2c"][:], X["bcsb"][:], 0.5), "sDVE")

        def stats_mms(o, first, last):
            def f(e, o=o, first=first, last=last):
                e.matmul(PS["psC"][0:1, 0:512], X["onesb"][:], xbf(o)[:, 0:512],
                         start=first, stop=last, skip_group_check=True)
                e.matmul(PS["psC"][0:1, 512:1024], X["onesb"][:], xbf(o)[:, 512:1024],
                         start=first, stop=last, skip_group_check=True)
                e.matmul(PS["psD"][0:1, 0:512], X["onesb"][:], xsq(o)[:, 0:512],
                         start=first, stop=last, skip_group_check=True)
                return e.matmul(PS["psD"][32:33, 0:512], X["onesb"][:],
                                xsq(o)[:, 512:1024],
                                start=first, stop=last, skip_group_check=True)
            v = op("tensor", f, "sMM")
            FREE["xbq%d" % (o % 2)] = ("sMM", v)
            return v

        def ln_finalize(with_wrb):
            ps = PS["psC"]
            wait("vector", "sMM", CNT["sMM"])
            op("vector", lambda e: e.tensor_scalar_mul(X["m_row"][:], ps[0:1, :], 1.0 / D))
            op("vector", lambda e: e.tensor_scalar_mul(
                X["ex2_row"][0:1, 0:512], PS["psD"][0:1, 0:512], 1.0 / D))
            op("vector", lambda e: e.tensor_scalar_mul(
                X["ex2_row"][0:1, 512:1024], PS["psD"][32:33, 0:512], 1.0 / D))
            op("vector", lambda e: e.tensor_mul(X["scr_row"][:], X["m_row"][:], X["m_row"][:]))
            op("vector", lambda e: e.tensor_sub(X["ex2_row"][:], X["ex2_row"][:],
                                                X["scr_row"][:]))
            op("vector", lambda e: e.tensor_scalar_add(X["ex2_row"][:], X["ex2_row"][:],
                                                       EPS), "sDVE")
            FREE["psC"] = ("sDVE", CNT["sDVE"])
            wait("scalar", "sDVE", CNT["sDVE"])
            op("scalar", lambda e: e.activation(X["ex2_row"][:], X["ex2_row"][:], AF.Ln))
            op("scalar", lambda e: e.activation(X["rstd_row"][:], X["ex2_row"][:], AF.Exp,
                                                scale=-0.5), "sACT")
            wait("gpsimd", "sACT", CNT["sACT"])
            dma("gpsimd", ap["rowsd"][0:1, :], X["m_row"][:], "sG")
            dma("gpsimd", ap["rowsd"][1:2, :], X["rstd_row"][:], "sG")
            wait("gpsimd", "sG", CNT["sG"])
            dma("gpsimd", X["mb"][:], ap["rowsd"][0:1, :].broadcast_to([128, TC]), "sG")
            g_done = dma("gpsimd", X["rb_"][:],
                         ap["rowsd"][1:2, :].broadcast_to([128, TC]), "sG")
            wait("vector", "sG", g_done)
            if with_wrb:
                op("vector", lambda e: e.tensor_mul(X["wrb"][:], X["rb_"][:], X["wb"][:]),
                   "sDVE")

        def ln_apply(src_ap, with_qsum, store_counts=None):
            strip_fin = []
            if store_counts is None:
                wait("sync", "sSTx0", CNT["sSTx0"])
                wait("sync", "sSTx1", CNT["sSTx1"])
            for s_ in range(ND):
                if store_counts is not None:
                    wait("sync", stx(s_), store_counts[s_])
                wfree(q="sync", name="xs%d" % (s_ % 2))
                lv = dma("sync", xs(s_)[:], src_ap[128 * s_:128 * (s_ + 1), :], xsem(s_))
                wait("vector", xsem(s_), lv)
                if with_qsum:
                    wfree("vector", "xnw%d" % (s_ % 2))
                op("vector", lambda e, s_=s_: e.tensor_sub(tmp(s_)[:], xs(s_)[:], X["mb"][:]))
                op("vector", lambda e, s_=s_: e.tensor_mul(
                    X["xln"][:, s_ * TC:(s_ + 1) * TC], tmp(s_)[:], X["rb_"][:]))
                if with_qsum:
                    op("vector", lambda e, s_=s_: e.tensor_mul(xnw(s_)[:], tmp(s_)[:],
                                                               X["wrb"][:]))
                    v = op("vector", lambda e, s_=s_: e.tensor_reduce(
                        X["qsum"][:, s_:s_ + 1], xnw(s_)[:], mybir.AxisListType.X,
                        ALU.add), "sDVE")
                else:
                    v = op("vector", lambda e: e.engine_nop(), "sDVE")
                FREE["xs%d" % (s_ % 2)] = ("sDVE", v)
                strip_fin.append(v)
            return strip_fin

        def ffn_phase(x_src_ap, x_dst_ap, w_src, car, do_stats, strips=None):
            store_counts = []
            wfree("tensor", "psC")
            if strips is None:
                wait("tensor", "sDVE", CNT["sDVE"])
                def tmm(e):
                    ins = None
                    for th in range(2):
                        for d_ in range(ND):
                            ins = e.matmul(PS["psC"][:, th * 512:(th + 1) * 512],
                                           X["abf"][:, d_ * 128:(d_ + 1) * 128],
                                           X["xln"][:, d_ * TC + th * 512: d_ * TC + th * 512 + 512],
                                           start=(d_ == 0), stop=(d_ == ND - 1),
                                           skip_group_check=True)
                    return ins
                op("tensor", tmm, "sMM")
            else:
                for d_ in range(ND):
                    wait("tensor", "sDVE", strips[d_])
                    def tmm1(e, d_=d_):
                        return e.matmul(PS["psC"][:, 0:512],
                                        X["abf"][:, d_ * 128:(d_ + 1) * 128],
                                        X["xln"][:, d_ * TC: d_ * TC + 512],
                                        start=(d_ == 0), stop=(d_ == ND - 1),
                                        skip_group_check=True)
                    op("tensor", tmm1, None)
                def tmm2(e):
                    ins = None
                    for d_ in range(ND):
                        ins = e.matmul(PS["psC"][:, 512:1024],
                                       X["abf"][:, d_ * 128:(d_ + 1) * 128],
                                       X["xln"][:, d_ * TC + 512: d_ * TC + 1024],
                                       start=(d_ == 0), stop=(d_ == ND - 1),
                                       skip_group_check=True)
                    return ins
                op("tensor", tmm2, "sMM")
            mm_t = CNT["sMM"]

            aro = ap["ar1o"] if car == "cAR1" else ap["ar2o"]
            def router_chain():
                wait("gpsimd", car, 1)
                gq = dma("gpsimd", X["qsum"][:], aro[:, :], "sG")
                wait("vector", "sG", gq)
                op("vector", lambda e: e.tensor_scalar_mul(X["qsumbf"][:], X["qsum"][:],
                                                           1.0 / NQMASK), "sDVE")
                wait("tensor", "sDVE", CNT["sDVE"])
                def lmm(e):
                    ins = None
                    for d_ in range(ND):
                        ins = e.matmul(PS["psD"][0:8, 0:1],
                                       X["rwbf"][:, d_ * 8:(d_ + 1) * 8],
                                       X["qsumbf"][:, d_:d_ + 1],
                                       start=(d_ == 0), stop=(d_ == ND - 1),
                                       skip_group_check=True)
                    return ins
                op("tensor", lmm, "sMM")
                wait("vector", "sMM", CNT["sMM"])
                op("vector", lambda e: e.tensor_add(X["lg"][:], PS["psD"][0:8, 0:1],
                                                    X["rbsb"][:]), "sDVE")
                wait("scalar", "sDVE", CNT["sDVE"])
                op("scalar", lambda e: e.activation(X["elg"][:], X["lg"][:], AF.Exp), "sACT")
                wait("vector", "sACT", CNT["sACT"])
                op("vector", lambda e: e.tensor_copy(X["elgbf"][:], X["elg"][:]), "sDVE")
                wait("tensor", "sDVE", CNT["sDVE"])
                op("tensor", lambda e: e.matmul(PS["psD"][0:1, 4:5], X["onesb"][0:8, 0:1],
                                                X["elgbf"][:], start=True, stop=True,
                                                skip_group_check=True), "sMM")
                wait("vector", "sMM", CNT["sMM"])
                op("vector", lambda e: e.reciprocal(X["rec8"][0:1, :], PS["psD"][0:1, 4:5]),
                   "sDVE")
                wait("gpsimd", "sDVE", CNT["sDVE"])
                dma("gpsimd", ap["rowsd"][2:3, 0:1], X["rec8"][0:1, :], "sG")
                wait("gpsimd", "sG", CNT["sG"])
                g8 = dma("gpsimd", X["rec8"][:], ap["rowsd"][2:3, 0:1].broadcast_to([8, 1]),
                         "sG")
                wait("vector", "sG", g8)
                op("vector", lambda e: e.tensor_scalar(X["rt2bf"][:], X["elg"][:],
                                                        X["rec8"][:, 0:1], 2.0,
                                                        ALU.mult, ALU.mult), "sDVE")
                wait("tensor", "sDVE", CNT["sDVE"])
                op("tensor", lambda e: e.matmul(PS["psD"][:, 8:9], X["eselbf"][:],
                                                X["rt2bf"][:], start=True, stop=True,
                                                skip_group_check=True), "sMM")
                wait("vector", "sMM", CNT["sMM"])
                op("vector", lambda e: e.tensor_copy(X["tcol"][:], PS["psD"][:, 8:9]))
                wait("vector", "sMM", mm_t)
                v = op("vector", lambda e: e.tensor_single_scalar(
                    X["tbf"][:], PS["psC"][:, :], X["tcol"][:, 0:1], ALU.mult), "sDVE")
                FREE["psC"] = ("sDVE", v)
                return v

            act_base = CNT["sACT"]
            for o in range(ND):
                wfree("gpsimd", "slab%d" % (o % 2))
                lw = dma("gpsimd", slab(o)[:].rearrange("p (c o) -> p c o", o=128),
                         ap[w_src].rearrange("(c p) o -> p c o", p=128)
                         [:, :, 128 * o:128 * (o + 1)], wsem(o))
                wfree("sync", "xs%d" % (o % 2))
                lx = dma("sync", xs(o)[:], x_src_ap[128 * o:128 * (o + 1), :], xsem(o))
                wait("tensor", wsem(o), lw)
                if o >= 2:
                    wait("tensor", "sACT", act_base + o)
                psn = "psA" if o % 2 == 0 else "psB"
                def base_only(e, o=o, psn=psn):
                    ins = None
                    for th in range(2):
                        for d_ in range(ND):
                            ins = e.matmul(PS[psn][:, th * 512:(th + 1) * 512],
                                     slab(o)[:, d_ * 128:(d_ + 1) * 128],
                                     X["xln"][:, d_ * TC + th * 512: d_ * TC + th * 512 + 512],
                                     start=(d_ == 0), stop=False, skip_group_check=True)
                    return ins
                def lora_only(e, o=o, psn=psn):
                    ins = None
                    for th in range(2):
                        ins = e.matmul(PS[psn][:, th * 512:(th + 1) * 512],
                                       X["btf"][:, 128 * o:128 * (o + 1)],
                                       X["tbf"][:, th * 512:(th + 1) * 512],
                                       start=False, stop=True, skip_group_check=True)
                    return ins
                def bmm(e, bo=base_only, lo=lora_only):
                    bo(e)
                    return lo(e)
                if o == 0:
                    op("tensor", base_only)
                    dve_t = router_chain()
                    wait("tensor", "sDVE", dve_t)
                    v = op("tensor", lora_only, "sMM")
                else:
                    v = op("tensor", bmm, "sMM")
                FREE["slab%d" % (o % 2)] = ("sMM", v)
                mm_o = v
                wait("scalar", "sMM", mm_o)
                wfree("scalar", "tmp%d" % (o % 2))
                va = op("scalar", lambda e, o=o, psn=psn: e.activation(
                    tmp(o)[:], PS[psn][:, :], AF.Identity,
                    bias=X["b2c"][:, o:o + 1], scale=0.5), "sACT")
                wait("vector", "sACT", va)
                wait("vector", xsem(o), lx)
                wfree("vector", "xnw%d" % (o % 2))
                vx = op("vector", lambda e, o=o: e.tensor_add(xnw(o)[:], xs(o)[:],
                                                              tmp(o)[:]), "sXD")
                FREE["xs%d" % (o % 2)] = ("sXD", vx)
                FREE["tmp%d" % (o % 2)] = ("sXD", vx)
                wait("sync", "sXD", vx)
                vst = dma("sync", x_dst_ap[128 * o:128 * (o + 1), :], xnw(o)[:], stx(o))
                FREE["xnw%d" % (o % 2)] = (stx(o), vst)
                store_counts.append(vst)
                if do_stats:
                    wfree("vector", "xbq%d" % (o % 2))
                    op("vector", lambda e, o=o: e.tensor_copy(xbf(o)[:], xnw(o)[:]))
                    op("vector", lambda e, o=o: e.tensor_mul(xsq(o)[:], xbf(o)[:],
                                                             xbf(o)[:]), "sDVE")
                    wait("tensor", "sDVE", CNT["sDVE"])
                    stats_mms(o, o == 0, o == ND - 1)
            return store_counts

        wait("tensor", "cst", cst_total)
        for s_ in range(ND):
            wfree("sync", "xs%d" % (s_ % 2))
            lv = dma("sync", xs(s_)[:], ap["xT"][128 * s_:128 * (s_ + 1), :], xsem(s_))
            wait("vector", xsem(s_), lv)
            wfree("vector", "xbq%d" % (s_ % 2))
            op("vector", lambda e, s_=s_: e.tensor_copy(xbf(s_)[:], xs(s_)[:]))
            v = op("vector", lambda e, s_=s_: e.tensor_mul(xsq(s_)[:], xbf(s_)[:],
                                                           xbf(s_)[:]), "sDVE")
            FREE["xs%d" % (s_ % 2)] = ("sDVE", v)
            wait("tensor", "sDVE", v)
            stats_mms(s_, s_ == 0, s_ == ND - 1)
        ln_finalize(True)
        strips0 = ln_apply(ap["xT"], True)
        wait("gpsimd", "sDVE", CNT["sDVE"])
        dma("gpsimd", ap["ar1i"][:, :], X["qsum"][:], "sG")
        wait("gpsimd", "sG", CNT["sG"])
        op("gpsimd", lambda e: e.collective_compute(
            "AllReduce", ALU.add, replica_groups=[list(range(NCORE))],
            ins=[ap["ar1i"].opt()], outs=[ap["ar1o"].opt()]), "cAR1")

        st1 = None
        if ph_max >= 1:
            st1 = ffn_phase(ap["xT"], ap["x1d"], "wfT", "cAR1", True,
                            strips=strips0)

        if ph_max >= 2:
            ln_finalize(False)
            strips2 = ln_apply(ap["x1d"], False, st1)
            dve_ln2 = CNT["sDVE"]

            for sec in range(2):
                wsrc = "wkT" if sec == 0 else "wvT"
                for g in range(NKV):
                    i = sec * NKV + g
                    wfree("gpsimd", "slab%d" % (i % 2))
                    lw = dma("gpsimd", slab(i)[:].rearrange("p (c o) -> p c o", o=128),
                             ap[wsrc].rearrange("(c p) o -> p c o", p=128)
                             [:, :, 128 * g:128 * (g + 1)], wsem(i))
                    wait("tensor", wsem(i), lw)
                    psn = "psA" if i % 2 == 0 else "psB"
                    wfree("tensor", psn)
                    def pmm(e, i=i, psn=psn):
                        ins = None
                        for th in range(2):
                            for d_ in range(ND):
                                ins = e.matmul(PS[psn][:, th * 512:(th + 1) * 512],
                                               slab(i)[:, d_ * 128:(d_ + 1) * 128],
                                               X["xln"][:, d_ * TC + th * 512: d_ * TC + th * 512 + 512],
                                               start=(d_ == 0), stop=(d_ == ND - 1),
                                               skip_group_check=True)
                        return ins
                    if i == 0:
                        for d_ in range(ND):
                            wait("tensor", "sDVE", strips2[d_])
                            def pm1(e, d_=d_):
                                return e.matmul(PS["psA"][:, 0:512],
                                                slab(0)[:, d_ * 128:(d_ + 1) * 128],
                                                X["xln"][:, d_ * TC: d_ * TC + 512],
                                                start=(d_ == 0), stop=(d_ == ND - 1),
                                                skip_group_check=True)
                            op("tensor", pm1, None)
                        def pm2(e):
                            ins = None
                            for d_ in range(ND):
                                ins = e.matmul(PS["psA"][:, 512:1024],
                                               slab(0)[:, d_ * 128:(d_ + 1) * 128],
                                               X["xln"][:, d_ * TC + 512: d_ * TC + 1024],
                                               start=(d_ == 0), stop=(d_ == ND - 1),
                                               skip_group_check=True)
                            return ins
                        v = op("tensor", pm2, "sMM")
                    else:
                        v = op("tensor", pmm, "sMM")
                    FREE["slab%d" % (i % 2)] = ("sMM", v)
                    mm_p = v
                    if sec == 0:
                        wait("vector", "sMM", mm_p)
                        wfree("vector", "evb%d" % (g % 2))
                        wfree("vector", "xbq%d" % (g % 2))
                        op("vector", lambda e, psn=psn: e.tensor_mul(
                            tmp(0)[:], PS[psn][:, :], X["s_cosk"][:]))
                        op("vector", lambda e, psn=psn: e.tensor_mul(
                            tmp(1)[0:64, :], PS[psn][64:128, :], X["s_sink"][0:64, :]))
                        op("vector", lambda e, psn=psn: e.tensor_mul(
                            tmp(1)[64:128, :], PS[psn][0:64, :], X["s_sink"][64:128, :]))
                        op("vector", lambda e, g=g: e.tensor_add(
                            evb(g)[:], tmp(0)[:], tmp(1)[:]))
                        wfree("vector", "xbq%d" % (g % 2))
                        v2 = op("vector", lambda e, g=g: e.tensor_mul(
                            xsq(g)[:], evb(g)[:], evb(g)[:]), "sDVE")
                        FREE[psn] = ("sDVE", v2)
                        wait("tensor", "sDVE", v2)
                        def knmm(e, g=g):
                            e.matmul(PS["psC"][0:1, 0:512], X["onesb"][:],
                                     xsq(g)[:, 0:512], start=True, stop=True,
                                     skip_group_check=True)
                            return e.matmul(PS["psC"][0:1, 512:1024], X["onesb"][:],
                                            xsq(g)[:, 512:1024], start=True, stop=True,
                                            skip_group_check=True)
                        vkn = op("tensor", knmm, "sMM")
                        FREE["xbq%d" % (g % 2)] = ("sMM", vkn)
                        wait("vector", "sMM", vkn)
                        op("vector", lambda e, g=g: e.tensor_reduce(
                            X["kmxr"][0:1, g:g + 1], PS["psC"][0:1, :],
                            mybir.AxisListType.X, ALU.max), "sDVE")
                    else:
                        wait("scalar", "sMM", mm_p)
                        wfree("scalar", "evb%d" % (g % 2))
                        va = op("scalar", lambda e, g=g, psn=psn: e.activation(
                            evb(g)[:], PS[psn][:, :], AF.Identity), "sACT")
                        FREE[psn] = ("sACT", va)
                        wait("tensor", "sACT", va)
                    def trs(e, g=g):
                        ins = None
                        for u in range(8):
                            ins = e.transpose(PS["psTR"][:, u * 128:(u + 1) * 128],
                                              evb(g)[:, u * 128:(u + 1) * 128],
                                              X["identb"][:])
                        return ins
                    vtr = op("tensor", trs, "sMM")
                    FREE["evb%d" % (g % 2)] = ("sMM", vtr)
                    wait("vector", "sMM", vtr)
                    wfree("vector", "natb%d" % (g % 2))
                    vnb = op("vector", lambda e, g=g: e.tensor_copy(natb(g)[:],
                                                                    PS["psTR"][:, :]),
                             "sDVE")
                    wait("sync", "sDVE", vnb)
                    vst = dma("sync",
                              ap["kvin"][(sec * NKV + g) * TC:(sec * NKV + g + 1) * TC, :]
                              .rearrange("(s p) h -> p s h", p=128),
                              natb(g)[:].rearrange("p (s h) -> p s h", h=128),
                              "sNB%d" % (g % 2))
                    FREE["natb%d" % (g % 2)] = ("sNB%d" % (g % 2), vst)
            wq_pre = {}
            for h0 in range(2):
                i0 = 2 * NKV + h0
                wfree("gpsimd", "slab%d" % (i0 % 2))
                wq_pre[h0] = dma("gpsimd",
                                 slab(i0)[:].rearrange("p (c o) -> p c o", o=128),
                                 ap["wqT"].rearrange("(c p) o -> p c o", p=128)
                                 [:, :, 128 * h0:128 * (h0 + 1)], wsem(i0))
            wait("gpsimd", "sNB0", CNT["sNB0"])
            wait("gpsimd", "sNB1", CNT["sNB1"])
            op("gpsimd", lambda e: e.collective_compute(
                "AllGather", ALU.bypass, replica_groups=[list(range(NCORE))],
                ins=[ap["kvin"].opt()], outs=[ap["kvout"].opt()]), "cAG")
            wait("gpsimd", "sDVE", CNT["sDVE"])
            dma("gpsimd", ap["arki"][0:1, :], X["kmxr"][0:1, 0:8], "sG")
            wait("gpsimd", "sG", CNT["sG"])
            op("gpsimd", lambda e: e.collective_compute(
                "AllReduce", ALU.max, replica_groups=[list(range(NCORE))],
                ins=[ap["arki"].opt()], outs=[ap["arko"].opt()]), "cARK")

            for h in range(NH):
                i = 2 * NKV + h
                if h < 2:
                    lw = wq_pre[h]
                else:
                    wfree("gpsimd", "slab%d" % (i % 2))
                    lw = dma("gpsimd", slab(i)[:].rearrange("p (c o) -> p c o", o=128),
                             ap["wqT"].rearrange("(c p) o -> p c o", p=128)
                             [:, :, 128 * h:128 * (h + 1)], wsem(i))
                wait("tensor", wsem(i), lw)
                psn = "psA" if i % 2 == 0 else "psB"
                wfree("tensor", psn)
                def qmm(e, i=i, psn=psn):
                    ins = None
                    for th in range(2):
                        for d_ in range(ND):
                            ins = e.matmul(PS[psn][:, th * 512:(th + 1) * 512],
                                           slab(i)[:, d_ * 128:(d_ + 1) * 128],
                                           X["xln"][:, d_ * TC + th * 512: d_ * TC + th * 512 + 512],
                                           start=(d_ == 0), stop=(d_ == ND - 1),
                                           skip_group_check=True)
                    return ins
                v = op("tensor", qmm, "sMM")
                FREE["slab%d" % (i % 2)] = ("sMM", v)
                wait("vector", "sMM", v)
                wfree("vector", "evb%d" % (h % 2))
                wfree("vector", "xbq%d" % (h % 2))
                op("vector", lambda e, psn=psn: e.tensor_mul(
                    tmp(0)[:], PS[psn][:, :], X["s_cosk"][:]))
                op("vector", lambda e, psn=psn: e.tensor_mul(
                    tmp(1)[0:64, :], PS[psn][64:128, :], X["s_sink"][0:64, :]))
                op("vector", lambda e, psn=psn: e.tensor_mul(
                    tmp(1)[64:128, :], PS[psn][0:64, :], X["s_sink"][64:128, :]))
                op("vector", lambda e, h=h: e.tensor_add(
                    evb(h)[:], tmp(0)[:], tmp(1)[:]))
                wfree("vector", "xbq%d" % (h % 2))
                vq = op("vector", lambda e, h=h: e.tensor_mul(
                    xsq(h)[:], evb(h)[:], evb(h)[:]), "sDVE")
                FREE[psn] = ("sDVE", vq)
                wait("sync", "sDVE", vq)
                vst = dma("sync", ap["qd"][128 * h:128 * (h + 1), :], evb(h)[:],
                          "sSTn%d" % (h % 2))
                FREE["evb%d" % (h % 2)] = ("sSTn%d" % (h % 2), vst)
                wait("tensor", "sDVE", vq)
                def qnmm(e, h=h):
                    e.matmul(PS["psC"][32:33, 0:512], X["onesb"][:],
                             xsq(h)[:, 0:512], start=True, stop=True,
                             skip_group_check=True)
                    return e.matmul(PS["psC"][32:33, 512:1024], X["onesb"][:],
                                    xsq(h)[:, 512:1024], start=True, stop=True,
                                    skip_group_check=True)
                vqn = op("tensor", qnmm, "sMM")
                FREE["xbq%d" % (h % 2)] = ("sMM", vqn)
                wait("vector", "sMM", vqn)
                for j in range(NJ):
                    op("vector", lambda e, h=h, j=j: e.tensor_reduce(
                        X["qmaxr"][0:1, h * 4 + j:h * 4 + j + 1],
                        PS["psC"][32:33, j * QT:(j + 1) * QT],
                        mybir.AxisListType.X, ALU.max),
                        "sDVE" if j == NJ - 1 else None)

        if ph_max >= 3:
            wait("gpsimd", "cARK", 1)
            gk = dma("gpsimd", X["kmxg"][0:1, 0:8], ap["arko"][0:1, :], "sG")
            wait("vector", "sG", gk)
            wait("vector", "sDVE", CNT["sDVE"])
            for g in range(NKV):
                for j in range(NJ):
                    idx = g * NJ + j
                    op("vector", lambda e, g=g, j=j, idx=idx: e.tensor_reduce(
                        X["c2a"][0:1, idx:idx + 1],
                        X["qmaxr"][0:1, :].rearrange("a (h b) -> a b h", b=4)
                        [:, j:j + 1, 4 * g:4 * g + 4],
                        mybir.AxisListType.X, ALU.max))
                    op("vector", lambda e, g=g, idx=idx: e.tensor_single_scalar(
                        X["c2b"][0:1, idx:idx + 1], X["c2a"][0:1, idx:idx + 1],
                        X["kmxg"][0:1, g:g + 1], ALU.mult),
                        "sDVE" if idx == NKV * NJ - 1 else None)
            wait("scalar", "sDVE", CNT["sDVE"])
            op("scalar", lambda e: e.activation(X["cpos"][:], X["c2b"][:], AF.Sqrt),
               "sACT")
            wait("vector", "sACT", CNT["sACT"])
            op("vector", lambda e: e.tensor_scalar_mul(X["cneg"][:], X["cpos"][:], -0.08838834764831845),
               "sDVE")
            wait("gpsimd", "sDVE", CNT["sDVE"])
            dma("gpsimd", ap["rowsd"][3:4, 0:32], X["cneg"][:], "sG")
            wait("gpsimd", "sG", CNT["sG"])
            gcb = dma("gpsimd", X["Cb"][:],
                      ap["rowsd"][3:4, 0:32].broadcast_to([128, 32]), "sG")
            wait("scalar", "sG", gcb)

            wait("sync", "cAG", 1)
            wait("sync", "sSTn0", CNT["sSTn0"])
            wait("sync", "sSTn1", CNT["sSTn1"])
            kvo = ap["kvout"].rearrange("(r blk row) h -> r blk row h",
                                        blk=2 * NKV, row=TC)

            gM = 0
            exp_of = {}
            pready = {}
            pv_of = {}
            smm_of = {}
            kload = {}
            copy_of = {}
            sup_desc = []
            for j_ in range(NJ):
                for g_ in range(NKV):
                    for u_ in range(4 * (j_ + 1)):
                        sup_desc.append((j_, g_, u_))
            NSUP = len(sup_desc)

            def emit_super_load(sg):
                j_, g_, u_ = sup_desc[sg]
                sl = sg % 4
                wfree("sync", "ktb%d" % sl)
                ktv = vtv = None
                for r2_ in range(2):
                    ktv = dma("sync",
                              X["ktb%d" % sl][64 * r2_:64 * (r2_ + 1), :]
                              .rearrange("p (kc h) -> p kc h", h=128),
                              kvo[r2_::2, g_, 64 * u_:64 * u_ + 64, :]
                              .rearrange("kc u_ h -> u_ kc h"),
                              "sKT%d" % sl)
                wfree("sync", "vtb%d" % sl)
                for r2_ in range(2):
                    vtv = dma("sync",
                              X["vtb%d" % sl][64 * r2_:64 * (r2_ + 1), :]
                              .rearrange("p (kc h) -> p kc h", h=128),
                              kvo[r2_::2, NKV + g_, 64 * u_:64 * u_ + 64, :]
                              .rearrange("kc u_ h -> u_ kc h"),
                              "sVT%d" % sl)
                kload[sg] = (sl, ktv, vtv)

            def emit_super_transpose(sg):
                sl, ktv, vtv = kload[sg]
                wait("tensor", "sKT%d" % sl, ktv)
                if sg >= 1:
                    wait("tensor", "sKTT", copy_of[sg - 1])
                def ktr(e, sl=sl):
                    ins = None
                    for kc_ in range(4):
                        ins = e.transpose(
                            PS["psTR"][:, kc_ * 128:(kc_ + 1) * 128],
                            X["ktb%d" % sl][:, kc_ * 128:(kc_ + 1) * 128],
                            X["identb"][:])
                    return ins
                vtr = op("tensor", ktr, "sTRK")
                FREE["ktb%d" % sl] = ("sTRK", vtr)
                wait("scalar", "sTRK", vtr)
                wfree("scalar", "ktT%d" % sl)
                copy_of[sg] = op("scalar", lambda e, sl=sl: e.activation(
                    X["ktT%d" % sl][:], PS["psTR"][:, 0:512], AF.Identity), "sKTT")

            emit_super_load(0)
            emit_super_transpose(0)
            gsup = 0
            for j in range(NJ):
                wfree("sync", "maskb")
                mload = dma("sync", X["maskb"][:], ap["masksd"][j], "sMK0")
                for g in range(NKV):
                    gi = j * NKV + g
                    CL = 16 * (j + 1)
                    wfree("sync", "qt%d" % (gi % 3))
                    lq = dma("sync", X["qt%d" % (gi % 3)][:].rearrange(
                        "p (b c) -> p b c", c=QT),
                        ap["qd"].rearrange("(b p) t -> p b t", p=128)
                        [:, 4 * g:4 * g + 4, QT * j:QT * (j + 1)],
                        "sQT%d" % (gi % 3))
                    first_chunk = gM

                    def emit_pv(M, cid, g=g, CL=CL):
                        sslot = kload[cid // 4][0]
                        psem, pval = pready[cid]
                        wait("tensor", psem, pval)
                        if M == 0:
                            wait("tensor", "sFIN", CNT["sFIN"])
                            wait("tensor", "sOEV", CNT["sOEV"])
                        def pvm(e, cid=cid, kc=cid % 4, M=M, CL=CL, sslot=sslot):
                            for b2 in range(2):
                                e.matmul(PS["psC"][:, b2 * 512:(b2 + 1) * 512],
                                         X["vtb%d" % sslot][:, kc * 128:(kc + 1) * 128],
                                         X["pbf%d" % (cid % 2)][:, b2 * 512:(b2 + 1) * 512],
                                         start=(M == 0), stop=(M == CL - 1),
                                         skip_group_check=True)
                            e.matmul(PS["psD"][0:1, 0:512], X["onesb"][:],
                                     X["pbf%d" % (cid % 2)][:, 0:512],
                                     start=(M == 0), stop=(M == CL - 1),
                                     skip_group_check=True)
                            return e.matmul(PS["psD"][32:33, 0:512], X["onesb"][:],
                                            X["pbf%d" % (cid % 2)][:, 512:1024],
                                            start=(M == 0), stop=(M == CL - 1),
                                            skip_group_check=True)
                        v = op("tensor", pvm, "sPV")
                        pv_of[cid] = v
                        if cid % 4 == 3:
                            FREE["vtb%d" % (kload[cid // 4][0])] = ("sPV", v)

                    for M in range(CL):
                        kc = M % 4
                        if kc == 0:
                            su = gsup
                            if su + 1 < NSUP:
                                emit_super_load(su + 1)
                                emit_super_transpose(su + 1)
                            gsup += 1
                        sps = "psA" if gM % 2 == 0 else "psB"
                        if gM < 2:
                            wfree("tensor", sps)
                        if kc == 0:
                            wait("tensor", "sKTT", copy_of[su])
                            wait("tensor", "sVT%d" % (kload[su][0]), kload[su][2])
                        if M == 0:
                            wait("tensor", "sQT%d" % (gi % 3), lq)
                        if gM >= 2:
                            wait("tensor", "sEXP", exp_of[gM - 2])
                        def smm(e, sps=sps, sslot=kload[su][0], kc=kc, gi=gi):
                            ins = None
                            for b2 in range(2):
                                ins = e.matmul(
                                    PS[sps][:, b2 * 512:(b2 + 1) * 512],
                                    X["ktT%d" % sslot][:, kc * 128:(kc + 1) * 128],
                                    X["qt%d" % (gi % 3)][:, b2 * 512:(b2 + 1) * 512],
                                    start=True, stop=True, skip_group_check=True)
                            return ins
                        vs = op("tensor", smm, "sSMM")
                        smm_of[gM] = vs
                        if kc == 3:
                            FREE["ktT%d" % (kload[su][0])] = ("sSMM", vs)
                        wait("scalar", "sSMM", vs)
                        if gM >= 2:
                            wait("scalar", "sPV", pv_of[gM - 2])
                        ve = op("scalar", lambda e, sps=sps, gM=gM, g=g, j=j:
                                e.activation(X["pbf%d" % (gM % 2)][:], PS[sps][:, :],
                                             AF.Exp, scale=0.08838834764831845,
                                             bias=X["Cb"][:, g * NJ + j:g * NJ + j + 1]),
                                "sEXP")
                        exp_of[gM] = ve
                        if M >= CL - 16:
                            moff = (M - (CL - 16)) * QT
                            wait("vector", "sEXP", ve)
                            if M == CL - 16 and g == 0:
                                wait("vector", "sMK0", mload)
                            def msk(e, gM=gM, moff=moff):
                                ins = None
                                for b in range(4):
                                    ins = e.tensor_mul(
                                        X["pbf%d" % (gM % 2)][:, b * QT:(b + 1) * QT],
                                        X["pbf%d" % (gM % 2)][:, b * QT:(b + 1) * QT],
                                        X["maskb"][:, moff:moff + QT])
                                return ins
                            vm = op("vector", msk, "sMSK")
                            pready[gM] = ("sMSK", vm)
                        else:
                            pready[gM] = ("sEXP", ve)
                        if M >= 1:
                            emit_pv(M - 1, gM - 1)
                        gM += 1
                    emit_pv(CL - 1, gM - 1)
                    FREE["qt%d" % (gi % 3)] = ("sSMM", smm_of[gM - 1])
                    if g == NKV - 1:
                        FREE["maskb"] = ("sMSK", CNT["sMSK"])
                    wait("scalar", "sPV", pv_of[gM - 1])
                    if gi >= 1:
                        wait("scalar", "sOTT", gi)
                    vo = op("scalar", lambda e: e.activation(
                        X["tbf"][:], PS["psC"][:, :], AF.Identity), "sOEV")
                    wait("vector", "sPV", pv_of[gM - 1])
                    op("vector", lambda e: e.reciprocal(X["scr_row"][0:1, 0:512],
                                                        PS["psD"][0:1, 0:512]))
                    op("vector", lambda e: e.reciprocal(X["scr_row"][0:1, 512:1024],
                                                        PS["psD"][32:33, 0:512]),
                       "sFIN")
                    wait("gpsimd", "sFIN", CNT["sFIN"])
                    dma("gpsimd", ap["rowsd"][2:3, :], X["scr_row"][:], "sG")
                    wait("gpsimd", "sG", CNT["sG"])
                    grb = dma("gpsimd", X["recb"][:],
                              ap["rowsd"][2:3, :].broadcast_to([128, TC]), "sG")
                    wait("vector", "sG", grb)
                    wait("vector", "sOEV", vo)
                    op("vector", lambda e, g=g, j=j: e.tensor_mul(
                        X["xln"][:].rearrange("p (hh t) -> p hh t", t=TC)
                        [:, 4 * g:4 * g + 4, QT * j:QT * (j + 1)],
                        X["tbf"][:].rearrange("p (h c) -> p h c", c=QT),
                        X["recb"][:].rearrange("p (h c) -> p h c", c=QT)), "sOTT")

            st2 = []
            fin_all = CNT["sOTT"]
            for o in range(ND):
                wfree("gpsimd", "slab%d" % (o % 2))
                lw = dma("gpsimd", slab(o)[:].rearrange("p (c o) -> p c o", o=128),
                         ap["woT"].rearrange("(c p) o -> p c o", p=128)
                         [:, :, 128 * o:128 * (o + 1)], wsem(o))
                if o == 0:
                    wait("sync", "sSTx0", CNT["sSTx0"])
                    wait("sync", "sSTx1", CNT["sSTx1"])
                wfree("sync", "xs%d" % (o % 2))
                lx = dma("sync", xs(o)[:], ap["x1d"][128 * o:128 * (o + 1), :], xsem(o))
                wait("tensor", wsem(o), lw)
                if o == 0:
                    wait("tensor", "sOTT", fin_all)
                psn = "psA" if o % 2 == 0 else "psB"
                wfree("tensor", psn)
                def wmm(e, o=o, psn=psn):
                    ins = None
                    for th in range(2):
                        for d_ in range(ND):
                            ins = e.matmul(PS[psn][:, th * 512:(th + 1) * 512],
                                           slab(o)[:, d_ * 128:(d_ + 1) * 128],
                                           X["xln"][:, d_ * TC + th * 512: d_ * TC + th * 512 + 512],
                                           start=(d_ == 0), stop=(d_ == ND - 1),
                                           skip_group_check=True)
                    return ins
                v = op("tensor", wmm, "sMM")
                FREE["slab%d" % (o % 2)] = ("sMM", v)
                wait("vector", "sMM", v)
                wait("vector", xsem(o), lx)
                wfree("vector", "xnw%d" % (o % 2))
                vx = op("vector", lambda e, o=o, psn=psn: e.tensor_add(
                    xnw(o)[:], PS[psn][:, :], xs(o)[:]), "sXD")
                FREE[psn] = ("sXD", vx)
                FREE["xs%d" % (o % 2)] = ("sXD", vx)
                wait("sync", "sXD", vx)
                vst = dma("sync", ap["x2d"][128 * o:128 * (o + 1), :], xnw(o)[:], stx(o))
                FREE["xnw%d" % (o % 2)] = (stx(o), vst)
                st2.append(vst)
                wfree("vector", "xbq%d" % (o % 2))
                op("vector", lambda e, o=o: e.tensor_copy(xbf(o)[:], xnw(o)[:]))
                op("vector", lambda e, o=o: e.tensor_mul(xsq(o)[:], xbf(o)[:], xbf(o)[:]),
                   "sDVE")
                wait("tensor", "sDVE", CNT["sDVE"])
                stats_mms(o, o == 0, o == ND - 1)

        if ph_max >= 4:
            ln_finalize(True)
            strips3 = ln_apply(ap["x2d"], True, st2)
            wait("gpsimd", "sDVE", CNT["sDVE"])
            dma("gpsimd", ap["ar2i"][:, :], X["qsum"][:], "sG")
            wait("gpsimd", "sG", CNT["sG"])
            op("gpsimd", lambda e: e.collective_compute(
                "AllReduce", ALU.add, replica_groups=[list(range(NCORE))],
                ins=[ap["ar2i"].opt()], outs=[ap["ar2o"].opt()]), "cAR2")
            ffn_phase(ap["x2d"], ap["outT"], "wfT", "cAR2", False,
                      strips=strips3)

        if os.environ.get("MK_DEBUG2") and ph_max >= 2:
            dout("dbg_qd", [NH * HD, TC], BF16)
            dout("dbg_kvin", [2 * NKV * TC, HD], BF16)
            dout("dbg_kmx", [1, 16])
            dout("dbg_qmax", [1, 128])
            dout("dbg_cb", [128, NJ * NKV])
            wait("gpsimd", "sSTn0", CNT["sSTn0"])
            wait("gpsimd", "sSTn1", CNT["sSTn1"])
            wait("gpsimd", "sNB0", CNT["sNB0"])
            wait("gpsimd", "sNB1", CNT["sNB1"])
            wait("gpsimd", "sDVE", CNT["sDVE"])
            dma("gpsimd", ap["dbg_qd"], ap["qd"], "sG")
            dma("gpsimd", ap["dbg_kvin"], ap["kvin"], "sG")
            dma("gpsimd", ap["dbg_kmx"], X["kmxg"][:] if ph_max >= 3 else X["kmxr"][:], "sG")
            dma("gpsimd", ap["dbg_qmax"], X["qmaxr"][:], "sG")
            if ph_max >= 3:
                dma("gpsimd", ap["dbg_cb"], X["Cb"][:], "sG")

        if os.environ.get("MK_DEBUG3") and ph_max >= 3:
            dout("dbg_obuf", [128, ND * TC], BF16)
            wait("gpsimd", "sOTT", CNT["sOTT"])
            dma("gpsimd", ap["dbg_obuf"], X["xln"][:], "sG")

        if os.environ.get("MK_DEBUG"):
            dbg = {}
            dout("dbg_rows", [4, TC]); dout("dbg_xln", [128, TC], BF16)
            dout("dbg_cols", [128, 8]); dout("dbg_tbf", [128, TC], BF16)
            dout("dbg_small", [8, 4])
            wait("gpsimd", "sDVE", CNT["sDVE"])
            wait("gpsimd", "sMM", CNT["sMM"])
            wait("gpsimd", "sACT", CNT["sACT"])
            dma("gpsimd", ap["dbg_rows"][0:1, :], X["m_row"][:], "sG")
            dma("gpsimd", ap["dbg_rows"][1:2, :], X["rstd_row"][:], "sG")
            dma("gpsimd", ap["dbg_rows"][2:3, :], X["ex2_row"][:], "sG")
            dma("gpsimd", ap["dbg_rows"][3:4, :], X["scr_row"][:], "sG")
            dma("gpsimd", ap["dbg_xln"], X["xln"][:, 0:TC], "sG")
            dma("gpsimd", ap["dbg_cols"][:, 0:1], X["tcol"][:], "sG", slow=True)
            dma("gpsimd", ap["dbg_cols"][:, 1:2], X["b2c"][:, 0:1], "sG", slow=True)
            dma("gpsimd", ap["dbg_cols"][:, 2:3], X["wb"][:, 0:1], "sG", slow=True)
            dma("gpsimd", ap["dbg_cols"][:, 3:4], X["wrb"][:, 0:1], "sG", slow=True)
            dma("gpsimd", ap["dbg_cols"][:, 4:5], X["qsum"][:, 0:1], "sG", slow=True)
            dma("gpsimd", ap["dbg_tbf"], X["tbf"][:], "sG")
            dma("gpsimd", ap["dbg_small"][:, 0:1], X["lg"][:], "sG", slow=True)
            dma("gpsimd", ap["dbg_small"][:, 1:2], X["elg"][:], "sG", slow=True)
            dma("gpsimd", ap["dbg_small"][:, 2:3], X["rt2"][:], "sG", slow=True)
            dma("gpsimd", ap["dbg_small"][:, 3:4], X["rec8"][:], "sG", slow=True)

        if ph_max < 4:
            src = ap["x1d"] if ph_max < 3 else ap["x2d"]
            wait("gpsimd", "sSTx0", CNT["sSTx0"])
            wait("gpsimd", "sSTx1", CNT["sSTx1"])
            wait("gpsimd", "sDVE", CNT["sDVE"])
            wait("gpsimd", "sMM", CNT["sMM"])
            dma("gpsimd", ap["outT"], src, "sG")

        wait("sync", "sSTx0", CNT["sSTx0"])
        wait("sync", "sSTx1", CNT["sSTx1"])
        wait("gpsimd", "sG", CNT["sG"])

        with nc.Block() as block:
            @block.tensor
            def _(t):
                for f in EV["tensor"]:
                    f(t)

            @block.scalar
            def _(s_):
                for f in EV["scalar"]:
                    f(s_)

            @block.vector
            def _(v):
                for f in EV["vector"]:
                    f(v)

            @block.sync
            def _(sy):
                for f in EV["sync"]:
                    f(sy)

            @block.gpsimd
            def _(g):
                for f in EV["gpsimd"]:
                    f(g)

        if os.environ.get("MK_VERBOSE"):
            print("[mk] events:", {k: len(v) for k, v in EV.items()})
    return nc


def _llama3_inv_freq():
    ROPE_THETA, ROPE_FACTOR, HIGH_FF, LOW_FF, OMP = 500000.0, 8.0, 4.0, 1.0, 8192
    inv = 1.0 / ROPE_THETA ** (np.arange(0, HD, 2, dtype=np.float32) / HD)
    wavelen = 2.0 * np.pi / inv
    low_wl = OMP / LOW_FF
    high_wl = OMP / HIGH_FF
    inv_scaled = np.where(wavelen > low_wl, inv / ROPE_FACTOR, inv)
    smooth = (OMP / wavelen - LOW_FF) / (HIGH_FF - LOW_FF)
    smoothed = (1.0 - smooth) / ROPE_FACTOR * inv + smooth * inv
    is_medium = (wavelen <= low_wl) & (wavelen >= high_wl)
    return np.where(is_medium, smoothed, inv_scaled).astype(np.float32)


def _own_idx(c):
    bo = np.arange(16)
    u = np.arange(64)
    return (64 * (c + 8 * bo[:, None]) + u[None, :]).reshape(-1)


def kernel(L0_attn_x, L0_attn_Wq, L0_attn_Wk, L0_attn_Wv, L0_attn_Wo,
           L0_ffn_W, L0_ffn_b, L0_ffn_A, L0_ffn_B,
           L0_ffn_router_W, L0_ffn_router_b):
    f32 = np.float32
    x = np.ascontiguousarray(np.asarray(L0_attn_x, dtype=f32)[0])
    shared = dict(
        wfT=np.ascontiguousarray(np.asarray(L0_ffn_W, f32).T),
        wqT=np.ascontiguousarray(np.asarray(L0_attn_Wq, f32).T),
        wkT=np.ascontiguousarray(np.asarray(L0_attn_Wk, f32).T),
        wvT=np.ascontiguousarray(np.asarray(L0_attn_Wv, f32).T),
        woT=np.ascontiguousarray(np.asarray(L0_attn_Wo, f32).T),
        bcols=np.ascontiguousarray(np.asarray(L0_ffn_b, f32).reshape(ND, 128).T),
        aT=np.ascontiguousarray(np.asarray(L0_ffn_A, f32).reshape(128, D).T),
        bT=np.ascontiguousarray(
            np.asarray(L0_ffn_B, f32).transpose(0, 2, 1).reshape(128, D)),
        rwT=np.ascontiguousarray(np.asarray(L0_ffn_router_W, f32).T),
        rb8=np.ascontiguousarray(np.asarray(L0_ffn_router_b, f32).reshape(8, 1)),
        esel=np.kron(np.eye(8, dtype=f32), np.ones((1, 16), f32)),
        ident=np.eye(128, dtype=f32),
    )
    invf = _llama3_inv_freq()
    scale = f32(1.0 / math.sqrt(HD))
    in_maps = []
    for c in range(NCORE):
        idx = _own_idx(c)
        pos = idx.astype(np.float32)
        fr = pos[None, :] * invf[:, None]
        cos = np.concatenate([np.cos(fr), np.cos(fr)], axis=0).astype(f32)
        sin = np.sin(fr).astype(f32)
        sin_rot = np.concatenate([-sin, sin], axis=0).astype(f32)
        wrowv = ((idx >= 611) & (idx <= 8190)).astype(f32).reshape(1, TC)
        masks = np.zeros((NJ, 128, 16 * QT), ml_dtypes.bfloat16)
        for j in range(NJ):
            qpos = idx[QT * j:QT * (j + 1)]
            for mloc in range(16):
                kpos = 2048 * j + 128 * mloc + np.arange(128)
                masks[j, :, mloc * QT:(mloc + 1) * QT] = (
                    kpos[:, None] <= qpos[None, :]).astype(ml_dtypes.bfloat16)
        m = dict(shared)
        m.update(
            xT=np.ascontiguousarray(x.T[:, idx]),
            cosk=np.ascontiguousarray(cos),
            sink=np.ascontiguousarray(sin_rot),
            wrow=wrowv,
            masksd=masks,
        )
        in_maps.append(m)

    nc = build(_PHASES)
    res = run_bass_kernel_spmd(nc, in_maps, core_ids=list(range(NCORE)))
    out = np.zeros((T, D), np.float32)
    for c in range(NCORE):
        out[_own_idx(c), :] = res.results[c]["outT"].T
    return out[None]


